# version 04
# baseline (speedup 1.0000x reference)
import math
import numpy as np
from contextlib import ExitStack

import concourse.bass as bass
import concourse.mybir as mybir
from concourse.alu_op_type import AluOpType as OP
from concourse.bass_utils import run_bass_kernel_spmd

F32 = mybir.dt.float32
BF16 = mybir.dt.bfloat16
AF = mybir.ActivationFunctionType
AX = mybir.AxisListType

B, S, D = 2, 8192, 1024
TPC, NT = 2048, 16
EPS = 1e-5
G = 91
NSEG = 18
NEG = -1e9

GRP_BATCH = [[0, 1, 2, 3], [4, 5, 6, 7]]
GRP_REV = [[3, 2, 1, 0], [7, 6, 5, 4]]


def lp_of_token(t):
    r = t // G
    if r % 2 == 0:
        return t
    return 182 * r + 90 - t


def build_lp2tok():
    lp2tok = np.empty(S, dtype=np.int64)
    for t in range(S):
        lp2tok[lp_of_token(t)] = t
    return lp2tok


def make_selo(q, lp2tok):
    selo = np.zeros((NT, 384, 128), np.float32)
    for i in range(NT):
        for p in range(128):
            t = 2048 * q + 128 * i + p
            lp = lp_of_token(t)
            j = lp // 128 - (16 * q - 1)
            jj = j - i
            assert 0 <= jj < 3, (q, i, p, jj)
            selo[i, jj * 128 + (lp % 128), p] = 1.0
    return selo


def make_xap_xkv(q, b, xin, lp2tok):
    xap = np.zeros((NSEG * 128, D), np.float32)
    xkv = np.zeros((NSEG * 64, D), np.float32)
    for j in range(NSEG):
        sg = 16 * q - 1 + j
        if sg < 0 or sg >= 64:
            continue
        toks = lp2tok[128 * sg:128 * sg + 128]
        xap[128 * j:128 * (j + 1)] = xin[b, toks]
        kv = toks[0::2]
        xkv[64 * j:64 * (j + 1)] = xin[b, kv]
    return xap, xkv


ENGS = ("tensor", "vector", "scalar", "gpsimd", "sync")


class P:
    def __init__(self, nc, block, ctx):
        import os
        self.limit = int(os.environ.get("PLIM", "1000000000"))
        self.ncalls = 0
        self.nc = nc
        self.block = block
        self.sem = {e: ctx.enter_context(nc.semaphore("S" + e)) for e in ENGS}
        self.dsem = {"sync": ctx.enter_context(nc.semaphore("Dsync")),
                     "gpsimd": ctx.enter_context(nc.semaphore("Dgp"))}
        self.csem = ctx.enter_context(nc.semaphore("Scc"))
        self.cnt = {e: 0 for e in ENGS}
        self.dcnt = {"sync": 0, "gpsimd": 0}
        self.ccnt = 0
        self.waited = {}

    def _waits(self, eng):
        out = []
        for e in ENGS:
            if e == eng:
                continue
            v = self.cnt[e]
            if v > self.waited.get((eng, "e" + e), 0):
                out.append((self.sem[e], v))
                self.waited[(eng, "e" + e)] = v
        for qn in ("sync", "gpsimd"):
            v = self.dcnt[qn]
            if v > self.waited.get((eng, "d" + qn), 0):
                out.append((self.dsem[qn], v))
                self.waited[(eng, "d" + qn)] = v
        v = self.ccnt
        if v > self.waited.get((eng, "cc"), 0):
            out.append((self.csem, v))
            self.waited[(eng, "cc")] = v
        return out

    def run(self, eng, fn):
        self.ncalls += 1
        if self.ncalls > self.limit:
            return
        waits = self._waits(eng)
        prev = self.cnt[eng]
        self.cnt[eng] += 1
        sem, val = self.sem[eng], self.cnt[eng]

        def body(e, waits=waits, fn=fn, sem=sem, prev=prev):
            if prev > 0:
                e.wait_ge(sem, prev)
            for s, v in waits:
                e.wait_ge(s, v)
            last = fn(e)
            last.then_inc(sem, 1)

        getattr(self.block, eng)(body)

    def dma(self, qn, out, in_):
        try:
            if out.tensor.dtype != in_.tensor.dtype:
                qn = "gpsimd"
        except AttributeError:
            pass
        self.ncalls += 1
        if self.ncalls > self.limit:
            return
        waits = self._waits(qn)
        self.dcnt[qn] += 16
        dsem = self.dsem[qn]

        def body(e, waits=waits, out=out, in_=in_, dsem=dsem):
            for s, v in waits:
                e.wait_ge(s, v)
            e.dma_start(out=out, in_=in_).then_inc(dsem, 16)

        getattr(self.block, qn)(body)

    def coll(self, kind, op, groups, ins, outs):
        self.ncalls += 1
        if self.ncalls > self.limit:
            return
        waits = self._waits("gpsimd")
        self.ccnt += 1
        csem = self.csem

        def body(e, waits=waits):
            for s, v in waits:
                e.wait_ge(s, v)
            e.collective_compute(kind, op, replica_groups=groups,
                                 ins=ins, outs=outs).then_inc(csem, 1)

        self.block.gpsimd(body)

    def finish(self):
        vs, vg, vc = self.dcnt["sync"], self.dcnt["gpsimd"], self.ccnt
        ss, sg, sc = self.dsem["sync"], self.dsem["gpsimd"], self.csem

        def body(e):
            e.wait_ge(ss, vs)
            e.wait_ge(sg, vg)
            if vc:
                e.wait_ge(sc, vc)

        self.block.gpsimd(body)


def MM(e, out, lhsT, rhs, start, stop):
    n = rhs.shape[-1]
    last = None
    for j in range(0, n, 512):
        w = min(512, n - j)
        last = e.matmul(out[:, j:j + w], lhsT, rhs[:, j:j + w],
                        start=start, stop=stop)
    return last


class Common:
    pass


def ln_tile(p, cm, src, dst, rows=128, affine=None):
    COL, DMP = cm.COL, cm.DMP
    r = rows

    p.run("vector", lambda e: e.reduce_sum(COL[0:r, 0:1], src, AX.X))
    p.run("scalar", lambda e: e.activation(COL[0:r, 1:2], COL[0:r, 0:1],
                                           AF.Identity, scale=1.0 / D))
    p.run("vector", lambda e: e.tensor_scalar(DMP[0:r, 0:1024], src,
                                              COL[0:r, 1:2], None, OP.subtract))
    p.run("scalar", lambda e: e.activation(DMP[0:r, 1024:2048], DMP[0:r, 0:1024],
                                           AF.Square, accum_out=COL[0:r, 2:3]))
    p.run("vector", lambda e: e.tensor_copy(COL[0:r, 3:4], COL[0:r, 2:3]))
    p.run("scalar", lambda e: e.activation(COL[0:r, 4:5], COL[0:r, 3:4],
                                           AF.Sqrt, bias=cm.EPSC[0:r, 0:1],
                                           scale=1.0 / D))
    p.run("vector", lambda e: e.reciprocal(COL[0:r, 5:6], COL[0:r, 4:5]))
    if affine is None:
        p.run("vector", lambda e: e.tensor_scalar(dst, DMP[0:r, 0:1024],
                                                  COL[0:r, 5:6], None, OP.mult))
    else:
        g_ap, b_ap = affine
        p.run("vector", lambda e: e.tensor_scalar(DMP[0:r, 1024:2048],
                                                  DMP[0:r, 0:1024],
                                                  COL[0:r, 5:6], None, OP.mult))
        p.run("vector", lambda e: e.tensor_tensor(DMP[0:r, 0:1024],
                                                  DMP[0:r, 1024:2048], g_ap,
                                                  OP.mult))
        p.run("vector", lambda e: e.tensor_tensor(dst, DMP[0:r, 0:1024], b_ap,
                                                  OP.add))


def tr8(p, cm, src_tm, dst_fm, PA):
    IDT = cm.IDT

    def t(e):
        last = None
        for c in range(8):
            last = e.matmul(PA[:, 128 * c:128 * (c + 1)],
                               src_tm[:, 128 * c:128 * (c + 1)], IDT[:], start=True, stop=True)
        return last
    p.run("tensor", t)
    p.run("vector", lambda e: e.tensor_copy(dst_fm, PA[:, 0:1024]))


def emit_sub1(p, cm, T, dram):
    nc = p.nc
    with ExitStack() as cx:
        sb = lambda n, sh, dt: cx.enter_context(nc.sbuf_tensor(n, sh, dt))
        WQKV = sb("s1_wqkv", [128, 24576], BF16)
        WOUT = sb("s1_wout", [128, 8192], BF16)
        SG = sb("s1_sg", [128, 8192], BF16)
        PA, PB = cm.PA, cm.PB
        TMP, COL = cm.TMP, cm.COL

        for pr in range(3):
            for dc in range(8):
                p.dma("sync", WQKV[:, (pr * 8 + dc) * 1024:(pr * 8 + dc + 1) * 1024],
                      T["wqkv"][128 * dc:128 * (dc + 1), 1024 * pr:1024 * (pr + 1)])
        for dc in range(8):
            p.dma("sync", WOUT[:, 1024 * dc:1024 * (dc + 1)],
                  T["wout1"][128 * dc:128 * (dc + 1), :])

        import os
        nseg_emit = int(os.environ.get("S1SEG", str(NSEG)))
        for j in range(nseg_emit):
            p.dma("sync", TMP[:], T["xap"][128 * j:128 * (j + 1), :])
            ln_tile(p, cm, TMP[:], SG[:, 0:1024])
            tr8(p, cm, SG[:, 0:1024], SG[:, 1024:2048], PA)
            p.dma("sync", TMP[0:64, :], T["xkv"][64 * j:64 * (j + 1), :])
            ln_tile(p, cm, TMP[0:64, :], SG[0:64, 2048:3072], rows=64)

            def tkv(e):
                last = None
                for c in range(8):
                    last = e.matmul(PA[:, 64 * c:64 * (c + 1)],
                                       SG[0:64, 2048 + 128 * c:2048 + 128 * (c + 1)],
                                       cm.IDT[0:64, 0:64], start=True, stop=True)
                return last
            p.run("tensor", tkv)
            p.run("vector", lambda e: e.tensor_copy(SG[:, 3072:3584], PA[:, 0:512]))

            def mq(e):
                last = None
                for dc in range(8):
                    last = MM(e, PB, SG[:, 1024 + 128 * dc:1024 + 128 * (dc + 1)],
                              WQKV[:, dc * 1024:(dc + 1) * 1024],
                              dc == 0, dc == 7)
                return last
            p.run("tensor", mq)
            p.run("vector", lambda e: e.tensor_copy(SG[:, 0:1024], PB[:]))

            def tq(e):
                last = None
                for h in range(16):
                    last = e.matmul(PA[0:64, 128 * h:128 * (h + 1)],
                                    SG[:, 64 * h:64 * (h + 1)],
                                    cm.IDT[:], start=True, stop=True)
                return last
            p.run("tensor", tq)
            p.run("vector", lambda e: e.tensor_copy(SG[0:64, 4096:6144],
                                                    PA[0:64, 0:2048]))

            def mk(e):
                last = None
                for dc in range(8):
                    last = MM(e, PB[0:64, :], SG[:, 3072 + 64 * dc:3072 + 64 * (dc + 1)],
                              WQKV[:, (8 + dc) * 1024:(9 + dc) * 1024],
                              dc == 0, dc == 7)
                return last
            p.run("tensor", mk)
            p.run("vector", lambda e: e.tensor_copy(SG[0:64, 6144:7168], PB[0:64, :]))

            def tk(e):
                last = None
                for h in range(16):
                    last = e.matmul(PA[0:64, 64 * h:64 * (h + 1)],
                                    SG[0:64, 6144 + 64 * h:6144 + 64 * (h + 1)],
                                    cm.IDT[0:64, 0:64], start=True, stop=True)
                return last
            p.run("tensor", tk)
            p.run("vector", lambda e: e.tensor_copy(SG[0:64, 1024:2048],
                                                    PA[0:64, 0:1024]))

            def mv(e):
                last = None
                for dc in range(8):
                    last = MM(e, PB[0:64, :], SG[:, 3072 + 64 * dc:3072 + 64 * (dc + 1)],
                              WQKV[:, (16 + dc) * 1024:(17 + dc) * 1024],
                              dc == 0, dc == 7)
                return last
            p.run("tensor", mv)
            p.run("vector", lambda e: e.tensor_copy(SG[0:64, 7168:8192], PB[0:64, :]))

            def ms(e):
                last = None
                for h in range(16):
                    last = e.matmul(PA[:, 64 * h:64 * (h + 1)],
                                    SG[0:64, 4096 + 128 * h:4096 + 128 * (h + 1)],
                                    SG[0:64, 1024 + 64 * h:1024 + 64 * (h + 1)],
                                    start=True, stop=True)
                return last
            p.run("tensor", ms)

            def sexp1(e):
                last = None
                for h in range(16):
                    last = e.activation(SG[:, 6144 + 64 * h:6144 + 64 * (h + 1)],
                                        PA[:, 64 * h:64 * (h + 1)], AF.Exp,
                                        scale=0.125, accum_out=COL[:, 8 + h:9 + h])
                return last
            p.run("scalar", sexp1)

            def vrc1(e):
                last = None
                for h in range(16):
                    last = e.reciprocal(COL[:, 24 + h:25 + h], COL[:, 8 + h:9 + h])
                return last
            p.run("vector", vrc1)

            def tp(e):
                last = None
                for h in range(16):
                    last = e.matmul(PA[0:64, 128 * h:128 * (h + 1)],
                                       SG[:, 6144 + 64 * h:6144 + 64 * (h + 1)],
                                       cm.IDT[:], start=True, stop=True)
                return last
            p.run("tensor", tp)
            p.run("vector", lambda e: e.tensor_copy(SG[0:64, 2048:4096], PA[0:64, 0:2048]))

            def mpv(e):
                last = None
                for h in range(16):
                    last = e.matmul(PB[:, 64 * h:64 * (h + 1)],
                                    SG[0:64, 2048 + 128 * h:2048 + 128 * (h + 1)],
                                    SG[0:64, 7168 + 64 * h:7168 + 64 * (h + 1)],
                                    start=True, stop=True)
                return last
            p.run("tensor", mpv)

            def ysc(e):
                last = None
                for h in range(16):
                    last = e.tensor_scalar(SG[:, 64 * h:64 * (h + 1)],
                                           PB[:, 64 * h:64 * (h + 1)],
                                           COL[:, 24 + h:25 + h], None, OP.mult)
                return last
            p.run("vector", ysc)
            tr8(p, cm, SG[:, 0:1024], SG[:, 1024:2048], PA)

            def mz(e):
                last = None
                for dc in range(8):
                    last = MM(e, PB, SG[:, 1024 + 128 * dc:1024 + 128 * (dc + 1)],
                              WOUT[:, 1024 * dc:1024 * (dc + 1)], dc == 0, dc == 7)
                return last
            p.run("tensor", mz)
            p.run("vector", lambda e: e.tensor_copy(SG[:, 5120:6144], PB[:]))
            p.dma("gpsimd", dram["zp"][128 * j:128 * (j + 1), :], SG[:, 5120:6144])

        for i in range(NT if nseg_emit == NSEG else 0):
            p.dma("sync", SG[:, 3072:3456].rearrange("p (a q) -> p a q", a=3),
                  T["selo"][i].rearrange("(a p) q -> p a q", p=128))
            for a in range(3):
                p.dma("sync", SG[:, 1024 * a:1024 * (a + 1)],
                      dram["zp"][128 * (i + a):128 * (i + a + 1), :])

            def mo(e, i=i):
                last = None
                for a in range(3):
                    last = MM(e, PB, SG[:, 3072 + 128 * a:3072 + 128 * (a + 1)],
                              SG[:, 1024 * a:1024 * (a + 1)], a == 0, a == 2)
                return last
            p.run("tensor", mo)
            p.dma("sync", TMP[:], T["x0"][128 * i:128 * (i + 1), :])
            p.run("vector", lambda e: e.tensor_tensor(cm.DMP[:, 0:1024], TMP[:],
                                                      PB[:], OP.add))
            p.dma("gpsimd", dram["xcur"][128 * i:128 * (i + 1), :], cm.DMP[:, 0:1024])


def emit_ffn(p, cm, T, dram, first):
    nc = p.nc
    with ExitStack() as cx:
        sb = lambda n, sh, dt: cx.enter_context(nc.sbuf_tensor(n, sh, dt))
        tag = "a" if first else "b"
        WA = sb("f_wa" + tag, [128, 32768], BF16)
        WB = sb("f_wb" + tag, [128, 32768], BF16)
        D2 = sb("f_d2" + tag, [128, 3072], F32)
        BH = sb("f_bh" + tag, [128, 2048], F32)
        XB = sb("f_xb" + tag, [128, 1024], F32)
        XN = sb("f_xn" + tag, [128, 10240], BF16)
        PA, PB = cm.PA, cm.PB
        TMP, COL, DMP = cm.TMP, cm.COL, cm.DMP

        if first:
            for wi, wt in ((0, T["w1t"]), (1, T["w2t"])):
                nch = 8 if wi == 0 else 32
                wid = 4096 if wi == 0 else 1024
                nstrip = (nch * wid) // 2048
                for st in range(nstrip):
                    r0 = (st * 2048) // wid
                    c0 = (st * 2048) % wid
                    if wid >= 2048:
                        p.dma("sync", DMP[:, 0:2048],
                              wt[128 * r0:128 * (r0 + 1), c0:c0 + 2048])
                    else:
                        p.dma("sync",
                              DMP[:, 0:2048].rearrange("p (a n) -> p a n", a=2),
                              wt[128 * r0:128 * (r0 + 2), :].rearrange(
                                  "(a p) n -> p a n", p=128))
                    col = 8 + wi * 20 + (st % 20)
                    if st % 20 == st // 1 * 0 + (st % 20):
                        pass
                    p.run("scalar", lambda e, col=col, fresh=(st < 20): e.activation(
                        D2[:, 0:2048], DMP[:, 0:2048], AF.Abs,
                        accum_out=COL[:, 48 + (0 if wi == 0 else 1):49 + (0 if wi == 0 else 1)]))
                    a_col = 50 + wi * 2
                    b_col = 51 + wi * 2
                    if st == 0:
                        p.run("vector", lambda e, a_col=a_col, wi=wi: e.tensor_copy(
                            COL[:, a_col:a_col + 1], COL[:, 48 + wi:49 + wi]))
                    else:
                        src_c = a_col if st % 2 == 1 else b_col
                        dst_c = b_col if st % 2 == 1 else a_col
                        p.run("vector", lambda e, s=src_c, d=dst_c, wi=wi:
                              e.tensor_tensor(COL[:, d:d + 1], COL[:, s:s + 1],
                                              COL[:, 48 + wi:49 + wi], OP.add))
                fin = (50 + wi * 2) if (nstrip - 1) % 2 == 0 else (51 + wi * 2)
                p.run("vector", lambda e, fin=fin, wi=wi: e.tensor_copy(
                    COL[:, 54 + wi:55 + wi], COL[:, fin:fin + 1]))
            ONE = sb("f_one", [128, 129], F32)
            p.run("vector", lambda e: e.memset(ONE[:], 1.0))
            for wi in range(2):
                p.run("tensor", lambda e, wi=wi: e.matmul(
                    cm.PC[0:1, 0:1], ONE[:, 0:1], COL[:, 54 + wi:55 + wi],
                    start=True, stop=True))
                p.run("vector", lambda e: e.tensor_copy(ONE[0:1, 128:129], cm.PC[0:1, 0:1]))
                p.run("tensor", lambda e: e.matmul(
                    cm.PC[:, 0:1], ONE[0:1, 0:128], ONE[0:1, 128:129],
                    start=True, stop=True))
                p.run("vector", lambda e, wi=wi: e.tensor_scalar_mul(
                    COL[:, 56 + wi:57 + wi], cm.PC[:, 0:1], 1.0 / (4096.0 * 1024.0)))
                p.run("vector", lambda e, wi=wi: e.tensor_scalar_max(
                    COL[:, 58 + wi:59 + wi], COL[:, 56 + wi:57 + wi], 1e-8))
                p.run("vector", lambda e, wi=wi: e.tensor_scalar_mul(
                    COL[:, 60 + wi:61 + wi], COL[:, 58 + wi:59 + wi], 0.5))

            for wi, wt, dst in ((0, T["w1t"], WA), (1, T["w2t"], WB)):
                nch = 8 if wi == 0 else 32
                wid = 4096 if wi == 0 else 1024
                for c in range(nch):
                    for s0 in range(0, wid, 1024):
                        p.dma("sync", DMP[:, 0:1024],
                              wt[128 * c:128 * (c + 1), s0:s0 + 1024])
                        p.run("scalar", lambda e: e.activation(
                            D2[:, 0:1024], DMP[:, 0:1024], AF.Sign))
                        p.run("vector", lambda e: e.tensor_tensor(
                            D2[:, 1024:2048], DMP[:, 0:1024], D2[:, 0:1024], OP.mult))
                        p.run("vector", lambda e, wi=wi: e.tensor_scalar(
                            D2[:, 2048:3072], D2[:, 1024:2048],
                            COL[:, 60 + wi:61 + wi], None, OP.is_ge))
                        p.run("vector", lambda e, wi=wi, c=c, s0=s0, dst=dst, wid=wid:
                              e.tensor_tensor(dst[:, c * wid + s0:c * wid + s0 + 1024],
                                              D2[:, 2048:3072], D2[:, 0:1024], OP.mult))
            p.dma("gpsimd", dram["tern1"], WA[:])
            p.dma("gpsimd", dram["tern2"], WB[:])
        else:
            p.dma("sync", WA[:], dram["tern1"])
            p.dma("sync", WB[:], dram["tern2"])

        p.dma("sync", XB[:], T["b2r"])

        for i in range(NT):
            p.dma("sync", TMP[:], dram["xcur"][128 * i:128 * (i + 1), :])
            ln_tile(p, cm, TMP[:], XN[:, 0:1024])
            tr8(p, cm, XN[:, 0:1024], XN[:, 1024:2048], PA)
            for half in range(2):
                p.dma("sync", BH[:], T["b1r"][:, 2048 * half:2048 * (half + 1)])

                def mh(e, half=half):
                    last = None
                    for dc in range(8):
                        last = MM(e, PA, XN[:, 1024 + 128 * dc:1024 + 128 * (dc + 1)],
                                  WA[:, 4096 * dc + 2048 * half:4096 * dc + 2048 * (half + 1)],
                                  dc == 0, dc == 7)
                    return last
                p.run("tensor", mh)
                p.run("vector", lambda e: e.tensor_tensor(D2[:, 0:2048], PA[:, 0:2048],
                                                          BH[:], OP.add))
                p.run("scalar", lambda e, half=half: e.activation(
                    XN[:, 2048 + 2048 * half:2048 + 2048 * (half + 1)],
                    D2[:, 0:2048], AF.Gelu))
            for g in range(2):
                def th(e, g=g):
                    last = None
                    for c in range(16):
                        last = e.matmul(PA[:, 128 * c:128 * (c + 1)],
                                           XN[:, 2048 + 2048 * g + 128 * c:
                                              2048 + 2048 * g + 128 * (c + 1)], cm.IDT[:], start=True, stop=True)
                    return last
                p.run("tensor", th)
                p.run("vector", lambda e, g=g: e.tensor_copy(
                    XN[:, 6144 + 2048 * g:6144 + 2048 * (g + 1)], PA[:, 0:2048]))

            def mo(e):
                last = None
                for fc in range(32):
                    last = MM(e, PB, XN[:, 6144 + 128 * fc:6144 + 128 * (fc + 1)],
                              WB[:, 1024 * fc:1024 * (fc + 1)], fc == 0, fc == 31)
                return last
            p.run("tensor", mo)
            p.run("vector", lambda e: e.tensor_tensor(D2[:, 2048:3072], PB[:],
                                                      XB[:], OP.add))
            p.run("vector", lambda e: e.tensor_tensor(DMP[:, 0:1024], D2[:, 2048:3072],
                                                      TMP[:], OP.add))
            p.dma("gpsimd", dram["xcur"][128 * i:128 * (i + 1), :], DMP[:, 0:1024])


def emit_sub3(p, cm, T, dram):
    nc = p.nc
    TMP, COL, DMP = cm.TMP, cm.COL, cm.DMP
    PA, PB = cm.PA, cm.PB
    with ExitStack() as cx:
        sb = lambda n, sh, dt: cx.enter_context(nc.sbuf_tensor(n, sh, dt))
        QF = cx.enter_context(nc.sbuf_tensor("s3_qf", [128, 16384], BF16))
        with ExitStack() as cy:
            sb2 = lambda n, sh, dt: cy.enter_context(nc.sbuf_tensor(n, sh, dt))
            W3 = sb2("s3_w", [128, 24576], BF16)
            BR = sb2("s3_br", [128, 3072], F32)
            EX = sb2("s3_ex", [128, 4096], BF16)
            for pi, wn in enumerate(("wq3", "wk3", "wv3")):
                for dc in range(8):
                    p.dma("sync", W3[:, (pi * 8 + dc) * 1024:(pi * 8 + dc + 1) * 1024],
                          T[wn][128 * dc:128 * (dc + 1), :])
            p.dma("sync", BR[:, 0:1024], T["bq3r"])
            p.dma("sync", BR[:, 1024:2048], T["bk3r"])
            p.dma("sync", BR[:, 2048:3072], T["bv3r"])
            for i in range(NT):
                p.dma("sync", TMP[:], dram["xcur"][128 * i:128 * (i + 1), :])
                ln_tile(p, cm, TMP[:], EX[:, 0:1024])
                tr8(p, cm, EX[:, 0:1024], EX[:, 1024:2048], PA)
                for pi in range(3):
                    def mp(e, pi=pi):
                        last = None
                        for dc in range(8):
                            last = MM(e, PB, EX[:, 1024 + 128 * dc:1024 + 128 * (dc + 1)],
                                      W3[:, (pi * 8 + dc) * 1024:(pi * 8 + dc + 1) * 1024],
                                      dc == 0, dc == 7)
                        return last
                    p.run("tensor", mp)
                    p.run("vector", lambda e, pi=pi: e.tensor_tensor(
                        DMP[:, 0:1024], PB[:], BR[:, 1024 * pi:1024 * (pi + 1)], OP.add))
                    p.run("vector", lambda e: e.tensor_copy(EX[:, 2048:3072],
                                                            DMP[:, 0:1024]))
                    if pi == 0:
                        tr8(p, cm, EX[:, 2048:3072], QF[:, 1024 * i:1024 * (i + 1)], PA)
                    elif pi == 1:
                        tr8(p, cm, EX[:, 2048:3072], EX[:, 3072:4096], PA)
                        for dc in range(8):
                            p.dma("gpsimd", dram["kT_own"][dc, :, 128 * i:128 * (i + 1)],
                                  EX[:, 3072 + 128 * dc:3072 + 128 * (dc + 1)])
                    else:
                        p.dma("gpsimd", dram["v_own"][128 * i:128 * (i + 1), :],
                              EX[:, 2048:3072])
        p.coll("AllGather", OP.bypass, GRP_BATCH,
               [dram["kT_own"].opt()], [dram["kT_all"].opt()])
        p.coll("AllGather", OP.bypass, GRP_BATCH,
               [dram["v_own"].opt()], [dram["v_all"].opt()])

        with ExitStack() as cy:
            KQ = cy.enter_context(nc.sbuf_tensor("s3_kq", [128, 16384], BF16))
            VQ = cy.enter_context(nc.sbuf_tensor("s3_vq", [128, 16384], BF16))
            ACC = cy.enter_context(nc.sbuf_tensor("s3_acc", [128, 16384], F32))
            EX = cy.enter_context(nc.sbuf_tensor("s3_ex2", [128, 4096], BF16))
            for s in range(4):
                p.dma("sync", KQ[:].rearrange("p (a n) -> p a n", a=8),
                      dram["kT_all"][s].rearrange("a p n -> p a n"))
                p.dma("sync", VQ[:].rearrange("p (a n) -> p a n", a=16),
                      dram["v_all"][s].rearrange("(a p) n -> p a n", p=128))
                for i in range(NT):
                    def msc(e, i=i):
                        last = None
                        for dc in range(8):
                            last = MM(e, PA, QF[:, 1024 * i + 128 * dc:1024 * i + 128 * (dc + 1)],
                                      KQ[:, 2048 * dc:2048 * (dc + 1)], dc == 0, dc == 7)
                        return last
                    p.run("tensor", msc)
                    p.run("scalar", lambda e: e.activation(
                        EX[:, 0:2048], PA[:, 0:2048], AF.Exp, scale=1.0 / 32.0,
                        accum_out=COL[:, 32:33]))
                    if s == 0:
                        p.run("vector", lambda e, i=i: e.tensor_copy(
                            COL[:, 64 + i:65 + i], COL[:, 32:33]))
                    else:
                        sc = (64 + i) if s % 2 == 1 else (96 + i)
                        dc_ = (96 + i) if s % 2 == 1 else (64 + i)
                        p.run("vector", lambda e, sc=sc, dc_=dc_: e.tensor_tensor(
                            COL[:, dc_:dc_ + 1], COL[:, sc:sc + 1], COL[:, 32:33], OP.add))

                    def tpt(e):
                        last = None
                        for c in range(16):
                            last = e.matmul(PA[:, 128 * c:128 * (c + 1)],
                                               EX[:, 128 * c:128 * (c + 1)], cm.IDT[:], start=True, stop=True)
                        return last
                    p.run("tensor", tpt)
                    p.run("vector", lambda e: e.tensor_copy(EX[:, 2048:4096],
                                                            PA[:, 0:2048]))

                    def mpv(e):
                        last = None
                        for c in range(16):
                            last = MM(e, PB, EX[:, 2048 + 128 * c:2048 + 128 * (c + 1)],
                                      VQ[:, 1024 * c:1024 * (c + 1)], c == 0, c == 15)
                        return last
                    p.run("tensor", mpv)
                    if s == 0:
                        p.run("vector", lambda e, i=i: e.tensor_copy(
                            ACC[:, 1024 * i:1024 * (i + 1)], PB[:]))
                    else:
                        p.run("vector", lambda e, i=i: e.tensor_tensor(
                            DMP[:, 0:1024], ACC[:, 1024 * i:1024 * (i + 1)], PB[:], OP.add))
                        p.run("vector", lambda e, i=i: e.tensor_copy(
                            ACC[:, 1024 * i:1024 * (i + 1)], DMP[:, 0:1024]))
            for i in range(NT):
                fin = 96 + i
                p.run("vector", lambda e, fin=fin: e.reciprocal(
                    COL[:, 33:34], COL[:, fin:fin + 1]))
                p.run("vector", lambda e, i=i: e.tensor_scalar(
                    DMP[:, 0:1024], ACC[:, 1024 * i:1024 * (i + 1)],
                    COL[:, 33:34], None, OP.mult))
                p.dma("sync", TMP[:], dram["xcur"][128 * i:128 * (i + 1), :])
                p.run("vector", lambda e: e.tensor_tensor(DMP[:, 1024:2048],
                                                          DMP[:, 0:1024], TMP[:], OP.add))
                p.dma("gpsimd", dram["xcur"][128 * i:128 * (i + 1), :], DMP[:, 1024:2048])


def emit_sub5(p, cm, T, dram):
    nc = p.nc
    TMP, COL, DMP = cm.TMP, cm.COL, cm.DMP
    PA, PB = cm.PA, cm.PB

    with ExitStack() as cx:
        sb = lambda n, sh, dt: cx.enter_context(nc.sbuf_tensor(n, sh, dt))
        GR = sb("m_gr", [128, 2048], F32)
        XJ = sb("m_xj", [128, 3072], BF16)
        p.dma("sync", GR[:, 0:1024], T["lngr"])
        p.dma("sync", GR[:, 1024:2048], T["lnbr"])
        for i in range(NT):
            p.dma("sync", TMP[:], dram["xcur"][128 * i:128 * (i + 1), :])
            ln_tile(p, cm, TMP[:], XJ[:, 0:1024])
            p.run("vector", lambda e: e.tensor_copy(cm.DMP[:, 2048:3072], XJ[:, 0:1024]))
            ln_tile(p, cm, cm.DMP[:, 2048:3072], XJ[:, 1024:2048],
                    affine=(GR[:, 0:1024], GR[:, 1024:2048]))
            p.dma("gpsimd", dram["ln0_own"][128 * i:128 * (i + 1), :], XJ[:, 0:1024])
            p.dma("gpsimd", dram["h_own"][128 * i:128 * (i + 1), :], XJ[:, 1024:2048])
    p.coll("AllGather", OP.bypass, GRP_BATCH,
           [dram["h_own"].opt()], [dram["h_all"].opt()])

    with ExitStack() as cx:
        sb = lambda n, sh, dt: cx.enter_context(nc.sbuf_tensor(n, sh, dt))
        WIN = sb("m_win", [128, 12288], BF16)
        HB = sb("m_hb", [128, 5120], BF16)
        FL = sb("m_fl", [128, 4], F32)
        JEX = sb("m_jex", [128, 128], BF16)
        p.dma("sync", FL[:], T["mflags"])
        p.dma("sync", JEX[:], T["jex"])
        for dc in range(8):
            p.dma("sync", WIN[:, 1024 * dc:1024 * (dc + 1)],
                  T["m_winx"][128 * dc:128 * (dc + 1), :])
            p.dma("sync", WIN[:, 8192 + 512 * dc:8192 + 512 * (dc + 1)],
                  T["m_winz"][128 * dc:128 * (dc + 1), :])
        for j in range(64):
            s, ii = j // 16, j % 16
            jm = 63 - j
            sm, iim = jm // 16, jm % 16
            p.dma("sync", HB[:, 0:1024], dram["h_all"][s, 128 * ii:128 * (ii + 1), :])
            p.dma("sync", HB[:, 1024:2048],
                  dram["h_all"][sm, 128 * iim:128 * (iim + 1), :])
            p.run("tensor", lambda e: MM(e, PB, JEX[:], HB[:, 1024:2048], True, True))
            p.run("vector", lambda e: e.tensor_copy(HB[:, 1024:2048], PB[:]))
            p.run("vector", lambda e: e.tensor_scalar(DMP[:, 0:1024], HB[:, 0:1024],
                                                      FL[:, 0:1], None, OP.mult))
            p.run("vector", lambda e: e.scalar_tensor_tensor(
                HB[:, 2048:3072], HB[:, 1024:2048], FL[:, 1:2], DMP[:, 0:1024],
                OP.mult, OP.add))
            tr8(p, cm, HB[:, 2048:3072], HB[:, 3072:4096], PA)

            def mx(e):
                last = None
                for mc in range(8):
                    for dc in range(8):
                        last = e.matmul(PA[:, 128 * mc:128 * (mc + 1)],
                                        WIN[:, 1024 * dc + 128 * mc:1024 * dc + 128 * (mc + 1)],
                                        HB[:, 3072 + 128 * dc:3072 + 128 * (dc + 1)],
                                        start=(dc == 0), stop=(dc == 7))
                return last
            p.run("tensor", mx)
            p.run("vector", lambda e: e.tensor_copy(HB[:, 4096:5120], PA[:, 0:1024]))
            for mc in range(8):
                p.dma("gpsimd", dram["xT_pre"][128 * mc:128 * (mc + 1),
                                               128 * j:128 * (j + 1)],
                      HB[:, 4096 + 128 * mc:4096 + 128 * (mc + 1)])

            def mzp(e):
                last = None
                for mc in range(4):
                    for dc in range(8):
                        last = e.matmul(PB[:, 128 * mc:128 * (mc + 1)],
                                        WIN[:, 8192 + 512 * dc + 128 * mc:
                                            8192 + 512 * dc + 128 * (mc + 1)],
                                        HB[:, 3072 + 128 * dc:3072 + 128 * (dc + 1)],
                                        start=(dc == 0), stop=(dc == 7))
                return last
            p.run("tensor", mzp)
            p.run("vector", lambda e: e.tensor_copy(HB[:, 4096:4608], PB[:, 0:512]))
            for mc in range(4):
                p.dma("gpsimd", dram["zT"][128 * mc:128 * (mc + 1), 128 * j:128 * (j + 1)],
                      HB[:, 4096 + 128 * mc:4096 + 128 * (mc + 1)])

    with ExitStack() as cx:
        sb = lambda n, sh, dt: cx.enter_context(nc.sbuf_tensor(n, sh, dt))
        CHB = sb("m_chb", [128, 8192], BF16)
        XP = sb("m_xp", [128, 8200], F32)
        CV1 = sb("m_cv1", [128, 8192], F32)
        CV2 = sb("m_cv2", [128, 8192], F32)
        WCV = sb("m_wcv", [128, 40], F32)
        p.dma("sync", WCV[:, 0:32], T["m_wcvr"])
        p.dma("sync", WCV[:, 32:40], T["m_bcvr"])
        for mc in range(8):
            p.dma("sync", CHB[:], dram["xT_pre"][128 * mc:128 * (mc + 1), :])
            p.run("vector", lambda e: e.memset(XP[:, 0:3], 0.0))
            p.run("vector", lambda e: e.tensor_copy(XP[:, 3:8195], CHB[:]))
            p.run("vector", lambda e, mc=mc: e.tensor_scalar(
                CV1[:], XP[:, 0:8192], WCV[:, 4 * mc:4 * mc + 1], None, OP.mult))
            for k in range(1, 4):
                src, dst = (CV1, CV2) if k % 2 == 1 else (CV2, CV1)
                p.run("vector", lambda e, mc=mc, k=k, src=src, dst=dst:
                      e.scalar_tensor_tensor(dst[:], XP[:, k:k + 8192],
                                             WCV[:, 4 * mc + k:4 * mc + k + 1],
                                             src[:], OP.mult, OP.add))
            p.run("vector", lambda e, mc=mc: e.tensor_scalar(
                CV1[:], CV2[:], WCV[:, 32 + mc:33 + mc], None, OP.add))
            p.run("scalar", lambda e: e.activation(CHB[:], CV1[:], AF.Silu))
            p.dma("gpsimd", dram["xcT"][128 * mc:128 * (mc + 1), :], CHB[:])

    with ExitStack() as cx:
        sb = lambda n, sh, dt: cx.enter_context(nc.sbuf_tensor(n, sh, dt))
        CHB = sb("m_chb2", [128, 8192], BF16)
        DA = sb("m_da", [64, 8192], F32)
        DB = sb("m_db", [64, 8192], F32)
        WX = sb("m_wxs", [128, 512], BF16)
        WDT = sb("m_wdts", [32, 512], BF16)
        BDT = sb("m_bdts", [128, 4], F32)
        DTB = sb("m_dtb", [32, 8192], BF16)
        BMB = sb("m_bmb", [16, 8192], BF16)
        CMB = sb("m_cmb", [16, 8192], BF16)
        EXQ = sb("m_exq", [128, 8192], BF16)
        for dc in range(8):
            p.dma("sync", WX[:, 64 * dc:64 * (dc + 1)],
                  T["m_wx"][128 * dc:128 * (dc + 1), :])
        p.dma("sync", WDT[:], T["m_wdt"])
        p.dma("sync", BDT[:], T["m_bdtr"])
        for mc in range(8):
            p.dma("sync", CHB[:], dram["xcT"][128 * mc:128 * (mc + 1), :])
            for tb in range(16):
                p.run("tensor", lambda e, mc=mc, tb=tb: e.matmul(
                    cm.PC[0:64, :], WX[:, 64 * mc:64 * (mc + 1)],
                    CHB[:, 512 * tb:512 * (tb + 1)], start=True, stop=True))
                if mc == 0:
                    p.run("vector", lambda e, tb=tb: e.tensor_copy(
                        DA[:, 512 * tb:512 * (tb + 1)], cm.PC[0:64, :]))
                else:
                    src, dst = (DA, DB) if mc % 2 == 1 else (DB, DA)
                    p.run("vector", lambda e, tb=tb, src=src, dst=dst:
                          e.tensor_tensor(dst[:, 512 * tb:512 * (tb + 1)],
                                          src[:, 512 * tb:512 * (tb + 1)],
                                          cm.PC[0:64, :], OP.add))
        FIN = DB
        p.dma("gpsimd", DTB[:], FIN[0:32, :])
        p.dma("gpsimd", BMB[:], FIN[32:48, :])
        p.dma("gpsimd", CMB[:], FIN[48:64, :])
        for mcl in range(4):
            for tb in range(16):
                p.run("tensor", lambda e, mcl=mcl, tb=tb: e.matmul(
                    cm.PC[:], WDT[:, 128 * mcl:128 * (mcl + 1)],
                    DTB[:, 512 * tb:512 * (tb + 1)], start=True, stop=True))
                p.run("scalar", lambda e, mcl=mcl, tb=tb: e.activation(
                    DMP[:, 0:512], cm.PC[:], AF.Exp, bias=BDT[:, mcl:mcl + 1],
                    scale=1.0))
                p.run("scalar", lambda e, tb=tb: e.activation(
                    EXQ[:, 512 * tb:512 * (tb + 1)], DMP[:, 0:512], AF.Ln,
                    bias=1.0))
            p.dma("gpsimd", dram["dtT"][128 * mcl:128 * (mcl + 1), :], EXQ[:])

        p.dma("gpsimd", dram["bcT"][0:16, :], BMB[:])
        p.dma("gpsimd", dram["bcT"][16:32, :], CMB[:])

    with ExitStack() as cy:
        sb2 = lambda n, sh, dt: cy.enter_context(nc.sbuf_tensor(n, sh, dt))
        DTC = sb2("m_dtc", [128, 8192], BF16)
        XC1 = sb2("m_xc1", [128, 8192], BF16)
        ZTC = sb2("m_ztc", [128, 8192], BF16)
        YC = sb2("m_yc", [128, 8192], F32)
        SC = sb2("m_sc", [128, 4096], F32)
        SCB = sb2("m_scb", [128, 1024], BF16)
        YGB = sb2("m_ygb", [128, 8192], BF16)
        REPS = sb2("m_repss", [128, 2048], BF16)
        RBS = sb2("m_rbss", [16, 128], BF16)
        SSUM = sb2("m_ssums", [128, 8], BF16)
        ALG = sb2("m_alg", [128, 192], F32)
        HST = sb2("m_hst", [128, 16], F32)
        FL = sb2("m_fl2", [128, 4], F32)
        DPC = sb2("m_dpc", [128, 4], F32)
        BMB = sb2("m_bmb2", [16, 8192], BF16)
        CMB = sb2("m_cmb2", [16, 8192], BF16)
        p.dma("sync", BMB[:], dram["bcT"][0:16, :])
        p.dma("sync", CMB[:], dram["bcT"][16:32, :])
        p.dma("sync", FL[:], T["mflags"])
        p.dma("sync", REPS[:].rearrange("p (a n) -> p a n", a=16),
              T["m_rep"].rearrange("a p n -> p a n"))
        p.dma("sync", RBS[:], T["m_rb"])
        p.dma("sync", SSUM[:], T["m_ssum"])
        p.dma("sync", ALG[:, 0:64], T["m_alogr"])
        p.dma("sync", DPC[:], T["m_dpr"])
        p.run("scalar", lambda e: e.activation(ALG[:, 64:128], ALG[:, 0:64], AF.Exp))
        p.run("vector", lambda e: e.tensor_scalar_mul(ALG[:, 128:192],
                                                      ALG[:, 64:128], -1.0))
        for mcl in range(4):
            p.dma("sync", DTC[:], dram["dtT"][128 * mcl:128 * (mcl + 1), :])
            p.dma("sync", XC1[:], dram["xcT"][128 * mcl:128 * (mcl + 1), :])
            p.dma("sync", YGB[:], dram["xcT"][512 + 128 * mcl:512 + 128 * (mcl + 1), :])
            p.run("vector", lambda e: e.tensor_scalar(SC[:, 0:4096], XC1[:, 0:4096],
                                                      FL[:, 2:3], None, OP.mult))
            p.run("vector", lambda e: e.scalar_tensor_tensor(
                XC1[:, 0:4096], YGB[:, 0:4096], FL[:, 3:4], SC[:, 0:4096],
                OP.mult, OP.add))
            p.run("vector", lambda e: e.tensor_scalar(SC[:, 0:4096], XC1[:, 4096:8192],
                                                      FL[:, 2:3], None, OP.mult))
            p.run("vector", lambda e: e.scalar_tensor_tensor(
                XC1[:, 4096:8192], YGB[:, 4096:8192], FL[:, 3:4], SC[:, 0:4096],
                OP.mult, OP.add))
            for gl in range(16):
                g = 16 * mcl + gl
                for tb in range(8):
                    t0, t1 = 1024 * tb, 1024 * (tb + 1)

                    def m1(e, gl=gl, t0=t0, t1=t1):
                        e.matmul(PA[:, 0:512], REPS[:, 128 * gl:128 * (gl + 1)],
                                 DTC[:, t0:t0 + 512], start=True, stop=True)
                        e.matmul(PA[:, 512:1024], REPS[:, 128 * gl:128 * (gl + 1)],
                                 DTC[:, t0 + 512:t1], start=True, stop=True)
                        e.matmul(PA[:, 1024:1536], REPS[:, 128 * gl:128 * (gl + 1)],
                                 XC1[:, t0:t0 + 512], start=True, stop=True)
                        e.matmul(PA[:, 1536:2048], REPS[:, 128 * gl:128 * (gl + 1)],
                                 XC1[:, t0 + 512:t1], start=True, stop=True)
                        e.matmul(PB[:, 0:512], RBS[:], BMB[:, t0:t0 + 512],
                                 start=True, stop=True)
                        return e.matmul(PB[:, 512:1024], RBS[:], BMB[:, t0 + 512:t1],
                                        start=True, stop=True)
                    p.run("tensor", m1)
                    p.run("scalar", lambda e, g=g: e.activation(
                        SC[:, 0:1024], PA[:, 0:1024], AF.Exp,
                        scale=ALG[:, 128 + g:129 + g]))
                    p.run("vector", lambda e: e.tensor_copy(SC[:, 1024:2048],
                                                            PB[:, 0:1024]))
                    p.run("vector", lambda e: e.tensor_tensor(
                        SC[:, 2048:3072], SC[:, 1024:2048], PA[:, 1024:2048], OP.mult))
                    p.run("vector", lambda e: e.tensor_tensor(
                        SC[:, 1024:2048], SC[:, 2048:3072], PA[:, 0:1024], OP.mult))
                    init = 0.0 if tb == 0 else HST[:, gl:gl + 1]
                    p.run("vector", lambda e, init=init: e.tensor_tensor_scan(
                        SC[:, 3072:4096], SC[:, 0:1024], SC[:, 1024:2048],
                        init, OP.mult, OP.add))
                    p.run("tensor", lambda e, t0=t0, t1=t1: (
                        e.matmul(PB[:, 0:512], RBS[:], CMB[:, t0:t0 + 512],
                                 start=True, stop=True),
                        e.matmul(PB[:, 512:1024], RBS[:], CMB[:, t0 + 512:t1],
                                 start=True, stop=True))[-1])
                    p.run("vector", lambda e, gl=gl: (
                        e.tensor_copy(HST[:, gl:gl + 1], SC[:, 4095:4096]),
                        e.tensor_tensor(SC[:, 0:1024], SC[:, 3072:4096],
                                        PB[:, 0:1024], OP.mult))[-1])
                    p.run("vector", lambda e: e.tensor_copy(SCB[:], SC[:, 0:1024]))
                    p.run("tensor", lambda e: (
                        e.matmul(cm.PC[0:8, :], SSUM[:], SCB[:, 0:512],
                                 start=True, stop=True),
                        e.matmul(cm.PD[0:8, :], SSUM[:], SCB[:, 512:1024],
                                 start=True, stop=True))[-1])
                    p.run("vector", lambda e: (
                        e.tensor_copy(SC[0:8, 3072:3584], cm.PC[0:8, :]),
                        e.tensor_copy(SC[0:8, 3584:4096], cm.PD[0:8, :]))[-1])
                    p.dma("gpsimd", YC[8 * gl:8 * gl + 8, t0:t0 + 512],
                          SC[0:8, 3072:3584])
                    p.dma("gpsimd", YC[8 * gl:8 * gl + 8, t0 + 512:t0 + 1024],
                          SC[0:8, 3584:4096])
            p.dma("sync", ZTC[:], dram["zT"][128 * mcl:128 * (mcl + 1), :])
            for tb in range(8):
                t0 = 1024 * tb
                p.run("scalar", lambda e, t0=t0: e.activation(
                    SC[:, 0:1024], ZTC[:, t0:t0 + 1024], AF.Silu))
                p.run("vector", lambda e, t0=t0, mcl=mcl: e.scalar_tensor_tensor(
                    SC[:, 1024:2048], XC1[:, t0:t0 + 1024], DPC[:, mcl:mcl + 1],
                    YC[:, t0:t0 + 1024], OP.mult, OP.add))
                p.run("vector", lambda e, t0=t0: e.tensor_tensor(
                    YGB[:, t0:t0 + 1024], SC[:, 1024:2048], SC[:, 0:1024], OP.mult))
            p.dma("gpsimd", dram["ygT"][128 * mcl:128 * (mcl + 1), :], YGB[:])

    with ExitStack() as cx:
        sb = lambda n, sh, dt: cx.enter_context(nc.sbuf_tensor(n, sh, dt))
        WOP = sb("m_wop", [128, 4096], BF16)
        YGS = sb("m_ygs", [128, 512], BF16)
        for mcl in range(4):
            p.dma("sync", WOP[:, 1024 * mcl:1024 * (mcl + 1)],
                  T["m_woutp"][128 * mcl:128 * (mcl + 1), :])
        for tb in range(64):
            for mcl in range(4):
                p.dma("sync", YGS[:, 128 * mcl:128 * (mcl + 1)],
                      dram["ygT"][128 * mcl:128 * (mcl + 1), 128 * tb:128 * (tb + 1)])

            def mw(e):
                last = None
                for mcl in range(4):
                    last = MM(e, PB, YGS[:, 128 * mcl:128 * (mcl + 1)],
                              WOP[:, 1024 * mcl:1024 * (mcl + 1)], mcl == 0, mcl == 3)
                return last
            p.run("tensor", mw)
            p.run("vector", lambda e: e.tensor_copy(DMP[:, 0:1024], PB[:]))
            p.dma("gpsimd", dram["mamb_in"][128 * tb:128 * (tb + 1), :], DMP[:, 0:1024])
    p.coll("ReduceScatter", OP.add, GRP_BATCH,
           [dram["mamb_in"].opt()], [dram["mamb_out"].opt()])

    with ExitStack() as cx:
        XJ = cx.enter_context(nc.sbuf_tensor("m_xj2", [128, 1024], BF16))
        for i in range(NT):
            p.dma("sync", TMP[:], dram["xcur"][128 * i:128 * (i + 1), :])
            p.dma("sync", DMP[:, 0:1024], dram["mamb_out"][128 * i:128 * (i + 1), :])
            p.dma("sync", XJ[:], dram["ln0_own"][128 * i:128 * (i + 1), :])
            p.run("vector", lambda e: e.tensor_copy(DMP[:, 1024:2048], XJ[:]))
            p.run("vector", lambda e: e.tensor_tensor(DMP[:, 2048:3072], DMP[:, 0:1024],
                                                      DMP[:, 1024:2048], OP.add))
            p.run("scalar", lambda e: e.activation(DMP[:, 0:1024], DMP[:, 2048:3072],
                                                   AF.Gelu))
            p.run("vector", lambda e: e.tensor_tensor(DMP[:, 1024:2048], DMP[:, 0:1024],
                                                      TMP[:], OP.add))
            p.dma("gpsimd", dram["xcur"][128 * i:128 * (i + 1), :], DMP[:, 1024:2048])


def emit_moe(p, cm, T, dram, first, final_out=None):
    nc = p.nc
    TMP, COL, DMP = cm.TMP, cm.COL, cm.DMP
    PA, PB, PC = cm.PA, cm.PB, cm.PC
    tag = "a" if first else "b"
    with ExitStack() as cx:
        sb = lambda n, sh, dt: cx.enter_context(nc.sbuf_tensor(n, sh, dt))
        WA = sb("e_wa" + tag, [128, 32768], BF16)
        XN = sb("e_xn" + tag, [128, 2048], BF16)
        RWA = sb("e_rw" + tag, [128, 128], F32)
        WRT = sb("e_wrt" + tag, [128, 64], BF16)
        BRR = sb("e_brr" + tag, [128, 8], F32)
        BEX = sb("e_bex" + tag, [16, 1024], BF16)
        RWT = sb("e_rwt" + tag, [16, 128], BF16)
        RW8 = sb("e_rw8" + tag, [128, 64], F32)
        for dc in range(8):
            p.dma("sync", WRT[:, 8 * dc:8 * (dc + 1)],
                  T["w_rT"][128 * dc:128 * (dc + 1), :])
        p.dma("sync", BRR[:], T["b_rr"])
        p.dma("sync", BEX[0:8, :], T["b_exp"])
        for grp in range(2):
            for ei in range(4):
                e_ = grp * 4 + ei
                for dc in range(8):
                    p.dma("sync", WA[:, (ei * 8 + dc) * 1024:(ei * 8 + dc + 1) * 1024],
                          T["w_expT"][e_, 128 * dc:128 * (dc + 1), :])
            for i in range(NT):
                p.dma("sync", TMP[:], dram["xcur"][128 * i:128 * (i + 1), :])
                ln_tile(p, cm, TMP[:], XN[:, 0:1024])
                tr8(p, cm, XN[:, 0:1024], XN[:, 1024:2048], PA)
                if grp == 0:
                    def mr(e):
                        last = None
                        for dc in range(8):
                            last = e.matmul(PC[:, 0:8],
                                            XN[:, 1024 + 128 * dc:1024 + 128 * (dc + 1)],
                                            WRT[:, 8 * dc:8 * (dc + 1)],
                                            start=(dc == 0), stop=(dc == 7))
                        return last
                    p.run("tensor", mr)

                    p.run("vector", lambda e: e.tensor_tensor(RW8[:, 0:8], PC[:, 0:8],
                                                              BRR[:], OP.add))
                    p.run("scalar", lambda e: e.activation(RW8[:, 8:16], RW8[:, 0:8],
                                                           AF.Exp))
                    p.run("vector", lambda e: e.reduce_sum(COL[:, 0:1], RW8[:, 8:16],
                                                           AX.X))
                    p.run("vector", lambda e: e.reciprocal(COL[:, 1:2], COL[:, 0:1]))
                    p.run("vector", lambda e: e.tensor_scalar(
                        RW8[:, 16:24], RW8[:, 8:16], COL[:, 1:2], None, OP.mult))
                    p.run("vector", lambda e: e.reduce_max(COL[:, 2:3], RW8[:, 16:24],
                                                           AX.X))
                    p.run("vector", lambda e: e.tensor_scalar(
                        RW8[:, 24:32], RW8[:, 16:24], COL[:, 2:3], None, OP.is_ge))
                    p.run("vector", lambda e: e.tensor_scalar_mul(
                        RW8[:, 32:40], RW8[:, 24:32], -1e9))
                    p.run("vector", lambda e: e.tensor_tensor(
                        RW8[:, 40:48], RW8[:, 16:24], RW8[:, 32:40], OP.add))
                    p.run("vector", lambda e: e.reduce_max(COL[:, 3:4], RW8[:, 40:48],
                                                           AX.X))
                    p.run("vector", lambda e: e.tensor_scalar(
                        RW8[:, 48:56], RW8[:, 16:24], COL[:, 3:4], None, OP.is_ge))
                    p.run("vector", lambda e, i=i: e.tensor_tensor(
                        RWA[:, 8 * i:8 * (i + 1)], RW8[:, 16:24], RW8[:, 48:56],
                        OP.mult))
                    cur = None
                else:
                    p.dma("sync", DMP[:, 2048:3072], dram["accd"][128 * i:128 * (i + 1), :])
                    cur = 2048
                for ei in range(4):
                    e_ = grp * 4 + ei

                    def me(e, ei=ei):
                        last = None
                        for dc in range(8):
                            last = MM(e, PB, XN[:, 1024 + 128 * dc:1024 + 128 * (dc + 1)],
                                      WA[:, (ei * 8 + dc) * 1024:(ei * 8 + dc + 1) * 1024],
                                      dc == 0, dc == 7)
                        return last
                    p.run("tensor", me)
                    if cur is None:
                        p.run("vector", lambda e, i=i, e_=e_: e.tensor_scalar(
                            DMP[:, 0:1024], PB[:], RWA[:, 8 * i + e_:8 * i + e_ + 1],
                            None, OP.mult))
                        cur = 0
                    else:
                        nxt = 1024 if cur != 1024 else 0
                        if grp == 1 and ei == 0:
                            nxt = 0
                        p.run("vector", lambda e, i=i, e_=e_, cur=cur, nxt=nxt:
                              e.scalar_tensor_tensor(
                                  DMP[:, nxt:nxt + 1024], PB[:],
                                  RWA[:, 8 * i + e_:8 * i + e_ + 1],
                                  DMP[:, cur:cur + 1024], OP.mult, OP.add))
                        cur = nxt
                if grp == 0:
                    p.dma("gpsimd", dram["accd"][128 * i:128 * (i + 1), :],
                          DMP[:, cur:cur + 1024])
                else:
                    p.run("vector", lambda e, i=i: e.tensor_copy(
                        XN[:, 1024:1032], RWA[:, 8 * i:8 * (i + 1)]))
                    p.run("tensor", lambda e: e.matmul(PC[0:8, 0:128],
                                                          XN[:, 1024:1032], cm.IDT[:], start=True, stop=True))
                    p.run("vector", lambda e: e.tensor_copy(RWT[0:8, :], PC[0:8, 0:128]))
                    p.run("tensor", lambda e: MM(e, PB, RWT[0:8, :], BEX[0:8, :],
                                                 True, True))
                    p.run("vector", lambda e, cur=cur: e.tensor_tensor(
                        DMP[:, 2048:3072], DMP[:, cur:cur + 1024], PB[:], OP.add))
                    p.run("vector", lambda e: e.tensor_tensor(
                        DMP[:, 0:1024], DMP[:, 2048:3072], TMP[:], OP.add))
                    dst = final_out if final_out is not None else dram["xcur"]
                    p.dma("gpsimd", dst[128 * i:128 * (i + 1), :], DMP[:, 0:1024])


def emit_sub7(p, cm, T, dram):
    nc = p.nc
    TMP, COL, DMP = cm.TMP, cm.COL, cm.DMP
    PA, PB, PC = cm.PA, cm.PB, cm.PC
    with ExitStack() as cx:
        sb = lambda n, sh, dt: cx.enter_context(nc.sbuf_tensor(n, sh, dt))
        KR = sb("a7_kr", [64, 6144], BF16)
        VR = sb("a7_vr", [128, 3072], BF16)
        W7 = sb("a7_w", [128, 32768], BF16)
        EX = sb("a7_ex", [128, 8192], BF16)
        HKH = sb("a7_hkh", [64, 4096], BF16)
        HVT = sb("a7_hvt", [128, 2048], BF16)
        GKH = sb("a7_gkh", [64, 512], BF16)
        GVT = sb("a7_gvt", [64, 512], BF16)
        MSK = sb("a7_msk", [128, 1216], F32)
        GLW = sb("a7_glw", [128, 2048], F32)
        LOC = sb("a7_loc", [128, 1024], F32)
        EFL = sb("a7_efl", [128, 4], F32)

        p.dma("sync", MSK[:, 0:384], T["bm"])
        p.dma("sync", MSK[:, 384:768], T["bm0"])
        p.dma("sync", MSK[:, 768:1152], T["bm1"])
        p.dma("sync", GLW[:, 0:1024], T["glwr"])
        p.dma("sync", GLW[:, 1024:2048], T["grwr"])
        p.dma("sync", EFL[:], T["eflags"])

        for dc in range(8):
            p.dma("sync", W7[:, 1024 * dc:1024 * (dc + 1)],
                  T["wk7"][128 * dc:128 * (dc + 1), :])
            p.dma("sync", W7[:, 8192 + 1024 * dc:8192 + 1024 * (dc + 1)],
                  T["wv7"][128 * dc:128 * (dc + 1), :])
        for i in range(NT):
            p.dma("sync", TMP[:], dram["xcur"][128 * i:128 * (i + 1), :])
            ln_tile(p, cm, TMP[:], EX[:, 2048:3072])
            tr8(p, cm, EX[:, 2048:3072], EX[:, 0:1024], PA)
            p.run("tensor", lambda e: (
                e.matmul(cm.PC[0:1, 0:512], cm.ONEB[:, 0:1],
                         EX[:, 2048:2560], start=True, stop=True),
                e.matmul(cm.PD[0:1, 0:512], cm.ONEB[:, 0:1],
                         EX[:, 2560:3072], start=True, stop=True))[-1])
            p.run("vector", lambda e: e.tensor_scalar_mul(
                DMP[0:1, 0:512], cm.PC[0:1, 0:512], 1.0 / 128))
            p.run("vector", lambda e: e.tensor_scalar_mul(
                DMP[0:1, 512:1024], cm.PD[0:1, 0:512], 1.0 / 128))
            p.run("vector", lambda e: e.tensor_copy(EX[0:1, 3072:4096],
                                                    DMP[0:1, 0:1024]))
            p.dma("gpsimd", dram["pooled_own"][i:i + 1, :], EX[0:1, 3072:4096])
            if i >= 14:
                for pi in range(2):
                    def mp(e, pi=pi):
                        last = None
                        for dc in range(8):
                            last = MM(e, PB, EX[:, 128 * dc:128 * (dc + 1)],
                                      W7[:, 8192 * pi + 1024 * dc:8192 * pi + 1024 * (dc + 1)],
                                      dc == 0, dc == 7)
                        return last
                    p.run("tensor", mp)
                    p.run("vector", lambda e, i=i, pi=pi: e.tensor_copy(
                        EX[:, 4096 + 2048 * (i - 14) + 1024 * pi:
                           4096 + 2048 * (i - 14) + 1024 * (pi + 1)], PB[:]))
        p.dma("gpsimd", dram["edges_own"][0:128, 0:1024], EX[:, 4096:5120])
        p.dma("gpsimd", dram["edges_own"][0:128, 1024:2048], EX[:, 5120:6144])
        p.dma("gpsimd", dram["edges_own"][128:256, 0:1024], EX[:, 6144:7168])
        p.dma("gpsimd", dram["edges_own"][128:256, 1024:2048], EX[:, 7168:8192])
        p.coll("AllGather", OP.bypass, GRP_BATCH,
               [dram["pooled_own"].opt()], [dram["pooled_all"].opt()])
        p.coll("AllGather", OP.bypass, GRP_BATCH,
               [dram["edges_own"].opt()], [dram["edges_all"].opt()])

        for dc in range(8):
            p.dma("sync", W7[:, 512 * dc:512 * (dc + 1)],
                  T["wgk7"][128 * dc:128 * (dc + 1), :])
            p.dma("sync", W7[:, 4096 + 512 * dc:4096 + 512 * (dc + 1)],
                  T["wgv7"][128 * dc:128 * (dc + 1), :])
        p.dma("sync", EX[0:64, 0:1024], dram["pooled_all"])

        def tpool(e):
            last = None
            for c in range(8):
                last = e.matmul(PA[:, 64 * c:64 * (c + 1)],
                                EX[0:64, 128 * c:128 * (c + 1)],
                                cm.IDT[0:64, 0:64], start=True, stop=True)
            return last
        p.run("tensor", tpool)
        p.run("vector", lambda e: e.tensor_copy(EX[:, 1024:1536], PA[:, 0:512]))
        for pi in range(2):
            def mg(e, pi=pi):
                last = None
                for dc in range(8):
                    last = MM(e, PB[0:64, 0:512], EX[:, 1024 + 64 * dc:1024 + 64 * (dc + 1)],
                              W7[:, 4096 * pi + 512 * dc:4096 * pi + 512 * (dc + 1)],
                              dc == 0, dc == 7)
                return last
            p.run("tensor", mg)
            if pi == 0:
                p.run("vector", lambda e: e.tensor_copy(EX[0:64, 1536:2048], PB[0:64, 0:512]))

                def tgk(e):
                    last = None
                    for lh in range(8):
                        last = e.matmul(PA[0:64, 64 * lh:64 * (lh + 1)],
                                        EX[0:64, 1536 + 64 * lh:1536 + 64 * (lh + 1)],
                                        cm.IDT[0:64, 0:64], start=True, stop=True)
                    return last
                p.run("tensor", tgk)
                p.run("vector", lambda e: e.tensor_copy(GKH[:], PA[0:64, 0:512]))
            else:
                p.run("vector", lambda e: e.tensor_copy(GVT[:], PB[0:64, 0:512]))
        for t in range(2):
            for s in range(4):
                p.dma("sync", EX[:, 2048 * (s % 2):2048 * (s % 2) + 2048],
                      dram["edges_all"][s, 128 * t:128 * (t + 1), :])
                if s == 0:
                    p.run("vector", lambda e: e.tensor_scalar(
                        DMP[:, 0:2048], EX[:, 0:2048], EFL[:, 0:1], None, OP.mult))
                else:
                    p.run("vector", lambda e, s=s: e.scalar_tensor_tensor(
                        DMP[:, 2048:4096] if s % 2 == 1 else DMP[:, 0:2048],
                        EX[:, 2048 * (s % 2):2048 * (s % 2) + 2048], EFL[:, s:s + 1],
                        DMP[:, 0:2048] if s % 2 == 1 else DMP[:, 2048:4096],
                        OP.mult, OP.add))
            p.run("vector", lambda e, t=t: e.tensor_copy(EX[:, 0:1024],
                                                         DMP[:, 2048:3072]))
            p.run("vector", lambda e, t=t: e.tensor_copy(
                HVT[:, 1024 * t:1024 * (t + 1)], DMP[:, 3072:4096]))

            def thk(e, t=t):
                last = None
                for h in range(16):
                    last = e.matmul(PA[0:64, 128 * h:128 * (h + 1)],
                                    EX[:, 64 * h:64 * (h + 1)],
                                    cm.IDT[:], start=True, stop=True)
                return last
            p.run("tensor", thk)
            p.run("vector", lambda e, t=t: e.tensor_copy(
                HKH[0:64, 2048 * t:2048 * (t + 1)], PA[0:64, 0:2048]))

        for dc in range(8):
            p.dma("sync", W7[:, 1024 * dc:1024 * (dc + 1)],
                  T["wq7"][128 * dc:128 * (dc + 1), :])
            p.dma("sync", W7[:, 8192 + 1024 * dc:8192 + 1024 * (dc + 1)],
                  T["wk7"][128 * dc:128 * (dc + 1), :])
            p.dma("sync", W7[:, 16384 + 1024 * dc:16384 + 1024 * (dc + 1)],
                  T["wv7"][128 * dc:128 * (dc + 1), :])
            p.dma("sync", W7[:, 24576 + 512 * dc:24576 + 512 * (dc + 1)],
                  T["wgq7"][128 * dc:128 * (dc + 1), :])
        WO7 = sb("a7_wo7", [128, 8192], BF16)
        for c in range(4):
            p.dma("sync", W7[:, 28672 + 1024 * c:28672 + 1024 * (c + 1)],
                  T["row7"][128 * c:128 * (c + 1), :])
        for dc in range(8):
            p.dma("sync", WO7[:, 1024 * dc:1024 * (dc + 1)],
                  T["wout7"][128 * dc:128 * (dc + 1), :])

        for i in range(NT):
            p.dma("sync", MSK[:, 1152:1216], T["lmask"][i])
            p.dma("sync", TMP[:], dram["xcur"][128 * i:128 * (i + 1), :])
            ln_tile(p, cm, TMP[:], EX[:, 6144:7168])
            tr8(p, cm, EX[:, 6144:7168], EX[:, 0:1024], PA)

            def mkr(e):
                last = None
                for dc in range(8):
                    last = MM(e, PB, EX[:, 128 * dc:128 * (dc + 1)],
                              W7[:, 8192 + 1024 * dc:8192 + 1024 * (dc + 1)],
                              dc == 0, dc == 7)
                return last
            p.run("tensor", mkr)
            p.run("vector", lambda e: e.tensor_copy(EX[:, 1024:2048], PB[:]))

            def tkr(e):
                last = None
                for h in range(16):
                    last = e.matmul(PA[0:64, 128 * h:128 * (h + 1)],
                                    EX[:, 1024 + 64 * h:1024 + 64 * (h + 1)],
                                    cm.IDT[:], start=True, stop=True)
                return last
            p.run("tensor", tkr)
            p.run("vector", lambda e, i=i: e.tensor_copy(
                KR[0:64, 2048 * (i % 3):2048 * (i % 3 + 1)], PA[0:64, 0:2048]))

            def mvr(e):
                last = None
                for dc in range(8):
                    last = MM(e, PB, EX[:, 128 * dc:128 * (dc + 1)],
                              W7[:, 16384 + 1024 * dc:16384 + 1024 * (dc + 1)],
                              dc == 0, dc == 7)
                return last
            p.run("tensor", mvr)
            p.run("vector", lambda e, i=i: e.tensor_copy(
                VR[:, 1024 * (i % 3):1024 * (i % 3 + 1)], PB[:]))

            def mq(e):
                last = None
                for dc in range(8):
                    last = MM(e, PB, EX[:, 128 * dc:128 * (dc + 1)],
                              W7[:, 1024 * dc:1024 * (dc + 1)], dc == 0, dc == 7)
                return last
            p.run("tensor", mq)
            p.run("vector", lambda e: e.tensor_tensor(DMP[:, 0:1024], PB[:],
                                                      GLW[:, 0:1024], OP.mult))
            p.run("vector", lambda e: e.tensor_reduce(
                COL[:, 0:16].rearrange("p (h o) -> p h o", o=1),
                DMP[:, 0:1024].rearrange("p (h k) -> p h k", k=64), AX.X, OP.add))
            p.run("vector", lambda e: e.tensor_tensor(DMP[:, 0:1024], PB[:],
                                                      GLW[:, 1024:2048], OP.mult))
            p.run("vector", lambda e: e.tensor_reduce(
                COL[:, 16:32].rearrange("p (h o) -> p h o", o=1),
                DMP[:, 0:1024].rearrange("p (h k) -> p h k", k=64), AX.X, OP.add))
            p.run("scalar", lambda e: e.activation(COL[:, 32:48], COL[:, 0:16],
                                                   AF.Sigmoid))
            p.run("scalar", lambda e: e.activation(COL[:, 48:64], COL[:, 16:32],
                                                   AF.Sigmoid))
            p.run("vector", lambda e: e.tensor_copy(EX[:, 1024:2048], PB[:]))

            def tqh(e):
                last = None
                for h in range(16):
                    last = e.matmul(PA[0:64, 128 * h:128 * (h + 1)],
                                    EX[:, 1024 + 64 * h:1024 + 64 * (h + 1)],
                                    cm.IDT[:], start=True, stop=True)
                return last
            p.run("tensor", tqh)
            p.run("vector", lambda e: e.tensor_copy(EX[0:64, 2048:4096],
                                                    PA[0:64, 0:2048]))

            moff = 384 if i == 0 else (768 if i == 1 else 0)
            ksrc, vsrc = [], []
            for a in range(3):
                bt = i - 2 + a
                if bt < 0:
                    ksrc.append((HKH, 2048 * (bt + 2)))
                    vsrc.append((HVT, 1024 * (bt + 2)))
                else:
                    ksrc.append((KR, 2048 * (bt % 3)))
                    vsrc.append((VR, 1024 * (bt % 3)))
            for hb in range(4):
                def msc(e, hb=hb, ksrc=ksrc):
                    last = None
                    for hh in range(4):
                        h = 4 * hb + hh
                        for a in range(3):
                            buf, off = ksrc[a]
                            last = e.matmul(
                                PA[:, 384 * hh + 128 * a:384 * hh + 128 * (a + 1)],
                                EX[0:64, 2048 + 128 * h:2048 + 128 * (h + 1)],
                                buf[0:64, off + 128 * h:off + 128 * (h + 1)],
                                start=True, stop=True)
                    return last
                p.run("tensor", msc)

                def vmask(e, moff=moff):
                    last = None
                    for hh in range(4):
                        last = e.tensor_tensor(DMP[:, 384 * hh:384 * (hh + 1)],
                                               PA[:, 384 * hh:384 * (hh + 1)],
                                               MSK[:, moff:moff + 384], OP.add)
                    return last
                p.run("vector", vmask)

                def sexp(e, hb=hb):
                    last = None
                    for hh in range(4):
                        last = e.activation(EX[:, 4096 + 384 * hh:4096 + 384 * (hh + 1)],
                                            DMP[:, 384 * hh:384 * (hh + 1)], AF.Exp,
                                            scale=0.125,
                                            accum_out=COL[:, 112 + 4 * hb + hh:113 + 4 * hb + hh])
                    return last
                p.run("scalar", sexp)

                def tpt(e):
                    last = None
                    for hh in range(4):
                        for a in range(3):
                            last = e.matmul(
                                PA[0:128, 512 * hh + 128 * a:512 * hh + 128 * (a + 1)],
                                EX[:, 4096 + 384 * hh + 128 * a:4096 + 384 * hh + 128 * (a + 1)],
                                cm.IDT[:], start=True, stop=True)
                    return last
                p.run("tensor", tpt)
                p.run("vector", lambda e: e.tensor_copy(EX[:, 5632:7680], PA[:, 0:2048]))

                def mpv(e, hb=hb, vsrc=vsrc):
                    last = None
                    for hh in range(4):
                        h = 4 * hb + hh
                        for a in range(3):
                            buf, off = vsrc[a]
                            last = e.matmul(PB[:, 64 * hh:64 * (hh + 1)],
                                            EX[:, 5632 + 512 * hh + 128 * a:
                                               5632 + 512 * hh + 128 * (a + 1)],
                                            buf[:, off + 64 * h:off + 64 * (h + 1)],
                                            start=(a == 0), stop=(a == 2))
                    return last
                p.run("tensor", mpv)

                def vrc(e, hb=hb):
                    last = None
                    for hh in range(4):
                        h = 4 * hb + hh
                        last = e.reciprocal(COL[:, 128 + h:129 + h],
                                            COL[:, 112 + h:113 + h])
                    return last
                p.run("vector", vrc)

                def vloc(e, hb=hb):
                    last = None
                    for hh in range(4):
                        h = 4 * hb + hh
                        last = e.tensor_scalar(LOC[:, 64 * h:64 * (h + 1)],
                                               PB[:, 64 * hh:64 * (hh + 1)],
                                               COL[:, 128 + h:129 + h], None, OP.mult)
                    return last
                p.run("vector", vloc)

            def mgq(e):
                last = None
                for dc in range(8):
                    last = MM(e, PB[:, 0:512], EX[:, 128 * dc:128 * (dc + 1)],
                              W7[:, 24576 + 512 * dc:24576 + 512 * (dc + 1)],
                              dc == 0, dc == 7)
                return last
            p.run("tensor", mgq)
            p.run("vector", lambda e: e.tensor_copy(EX[:, 1024:1536], PB[:, 0:512]))

            def tgq(e):
                last = None
                for lh in range(8):
                    last = e.matmul(PA[0:64, 128 * lh:128 * (lh + 1)],
                                    EX[:, 1024 + 64 * lh:1024 + 64 * (lh + 1)],
                                    cm.IDT[:], start=True, stop=True)
                return last
            p.run("tensor", tgq)
            p.run("vector", lambda e: e.tensor_copy(EX[0:64, 2048:3072],
                                                    PA[0:64, 0:1024]))

            def mls(e):
                last = None
                for lh in range(8):
                    last = e.matmul(PA[:, 64 * lh:64 * (lh + 1)],
                                    EX[0:64, 2048 + 128 * lh:2048 + 128 * (lh + 1)],
                                    GKH[0:64, 64 * lh:64 * (lh + 1)],
                                    start=True, stop=True)
                return last
            p.run("tensor", mls)

            def vlm(e):
                last = None
                for lh in range(8):
                    last = e.tensor_tensor(DMP[:, 64 * lh:64 * (lh + 1)],
                                           PA[:, 64 * lh:64 * (lh + 1)],
                                           MSK[:, 1152:1216], OP.add)
                return last
            p.run("vector", vlm)

            def slexp(e):
                last = None
                for lh in range(8):
                    last = e.activation(EX[:, 3072 + 64 * lh:3072 + 64 * (lh + 1)],
                                        DMP[:, 64 * lh:64 * (lh + 1)], AF.Exp,
                                        scale=0.125, accum_out=COL[:, 96 + lh:97 + lh])
                return last
            p.run("scalar", slexp)

            def tlp(e):
                last = None
                for lh in range(8):
                    last = e.matmul(PA[0:64, 128 * lh:128 * (lh + 1)],
                                    EX[:, 3072 + 64 * lh:3072 + 64 * (lh + 1)],
                                    cm.IDT[:], start=True, stop=True)
                return last
            p.run("tensor", tlp)
            p.run("vector", lambda e: e.tensor_copy(EX[0:64, 4096:5120], PA[0:64, 0:1024]))

            def mlv(e):
                last = None
                for lh in range(8):
                    last = e.matmul(PB[:, 512 + 64 * lh:512 + 64 * (lh + 1)],
                                    EX[0:64, 4096 + 128 * lh:4096 + 128 * (lh + 1)],
                                    GVT[0:64, 64 * lh:64 * (lh + 1)],
                                    start=True, stop=True)
                return last
            p.run("tensor", mlv)

            def vrr(e):
                last = None
                for lh in range(8):
                    last = e.reciprocal(COL[:, 144 + lh:145 + lh],
                                        COL[:, 96 + lh:97 + lh])
                return last
            p.run("vector", vrr)

            def vrem(e):
                last = None
                for lh in range(8):
                    last = e.tensor_scalar(EX[:, 1024 + 64 * lh:1024 + 64 * (lh + 1)],
                                           PB[:, 512 + 64 * lh:512 + 64 * (lh + 1)],
                                           COL[:, 144 + lh:145 + lh], None, OP.mult)
                return last
            p.run("vector", vrem)

            def trm(e):
                last = None
                for c in range(4):
                    last = e.matmul(PA[:, 128 * c:128 * (c + 1)],
                                    EX[:, 1024 + 128 * c:1024 + 128 * (c + 1)],
                                    cm.IDT[:], start=True, stop=True)
                return last
            p.run("tensor", trm)
            p.run("vector", lambda e: e.tensor_copy(EX[:, 1536:2048], PA[:, 0:512]))

            def mro(e):
                last = None
                for c in range(4):
                    last = MM(e, PB, EX[:, 1536 + 128 * c:1536 + 128 * (c + 1)],
                              W7[:, 28672 + 1024 * c:28672 + 1024 * (c + 1)],
                              c == 0, c == 3)
                return last
            p.run("tensor", mro)

            def vcmb1(e):
                last = None
                for h in range(16):
                    last = e.tensor_scalar(DMP[:, 64 * h:64 * (h + 1)],
                                           LOC[:, 64 * h:64 * (h + 1)],
                                           COL[:, 32 + h:33 + h], None, OP.mult)
                return last
            p.run("vector", vcmb1)

            def vcmb2(e):
                last = None
                for h in range(16):
                    last = e.scalar_tensor_tensor(
                        DMP[:, 1024 + 64 * h:1024 + 64 * (h + 1)],
                        PB[:, 64 * h:64 * (h + 1)], COL[:, 48 + h:49 + h],
                        DMP[:, 64 * h:64 * (h + 1)], OP.mult, OP.add)
                return last
            p.run("vector", vcmb2)
            p.run("vector", lambda e: e.tensor_copy(EX[:, 1024:2048], DMP[:, 1024:2048]))
            tr8(p, cm, EX[:, 1024:2048], EX[:, 0:1024], PA)

            def mwo(e):
                last = None
                for dc in range(8):
                    last = MM(e, PB, EX[:, 128 * dc:128 * (dc + 1)],
                              WO7[:, 1024 * dc:1024 * (dc + 1)], dc == 0, dc == 7)
                return last
            p.run("tensor", mwo)
            p.run("vector", lambda e: e.tensor_tensor(DMP[:, 0:1024], PB[:], TMP[:],
                                                      OP.add))
            p.dma("gpsimd", dram["xcur"][128 * i:128 * (i + 1), :], DMP[:, 0:1024])


def build_nc(nsub=8):
    nc = bass.Bass("TRN2", target_bir_lowering=False)

    def inp(name, shape, dt=F32):
        return nc.dram_tensor(name, list(shape), dt, kind="ExternalInput").ap()

    T = {}
    T["x0"] = inp("x0", (TPC, D))
    T["xap"] = inp("xap", (NSEG * 128, D))
    T["xkv"] = inp("xkv", (NSEG * 64, D))
    T["selo"] = inp("selo", (NT, 384, 128), BF16)
    T["wqkv"] = inp("wqkv", (D, 3 * D))
    T["wout1"] = inp("wout1", (D, D))
    T["w1t"] = inp("w1t", (D, 4096))
    T["b1r"] = inp("b1r", (128, 4096))
    T["w2t"] = inp("w2t", (4096, D))
    T["b2r"] = inp("b2r", (128, D))
    T["wq3"] = inp("wq3", (D, D))
    T["wk3"] = inp("wk3", (D, D))
    T["wv3"] = inp("wv3", (D, D))
    T["bq3r"] = inp("bq3r", (128, D))
    T["bk3r"] = inp("bk3r", (128, D))
    T["bv3r"] = inp("bv3r", (128, D))
    T["ident"] = inp("ident", (128, 128), BF16)
    T["jex"] = inp("jex", (128, 128), BF16)
    T["lngr"] = inp("lngr", (128, D))
    T["lnbr"] = inp("lnbr", (128, D))
    T["m_winx"] = inp("m_winx", (D, 1024))
    T["m_winz"] = inp("m_winz", (D, 512))
    T["m_wcvr"] = inp("m_wcvr", (128, 32))
    T["m_bcvr"] = inp("m_bcvr", (128, 8))
    T["m_wx"] = inp("m_wx", (D, 64))
    T["m_wdt"] = inp("m_wdt", (32, 512))
    T["m_bdtr"] = inp("m_bdtr", (128, 4))
    T["m_alogr"] = inp("m_alogr", (128, 64))
    T["m_dpr"] = inp("m_dpr", (128, 4))
    T["m_woutp"] = inp("m_woutp", (512, 1024))
    T["m_rep"] = inp("m_rep", (16, 128, 128), BF16)
    T["m_rb"] = inp("m_rb", (16, 128), BF16)
    T["m_ssum"] = inp("m_ssum", (128, 8), BF16)
    T["mflags"] = inp("mflags", (128, 4))
    T["w_expT"] = inp("w_expT", (8, D, D))
    T["b_exp"] = inp("b_exp", (8, D))
    T["w_rT"] = inp("w_rT", (D, 8))
    T["b_rr"] = inp("b_rr", (128, 8))
    T["wq7"] = inp("wq7", (D, D))
    T["wk7"] = inp("wk7", (D, D))
    T["wv7"] = inp("wv7", (D, D))
    T["wgq7"] = inp("wgq7", (D, 512))
    T["wgk7"] = inp("wgk7", (D, 512))
    T["wgv7"] = inp("wgv7", (D, 512))
    T["row7"] = inp("row7", (512, D))
    T["wout7"] = inp("wout7", (D, D))
    T["glwr"] = inp("glwr", (128, D))
    T["grwr"] = inp("grwr", (128, D))
    T["bm"] = inp("bm", (128, 384))
    T["bm0"] = inp("bm0", (128, 384))
    T["bm1"] = inp("bm1", (128, 384))
    T["lmask"] = inp("lmask", (NT, 128, 64))
    T["eflags"] = inp("eflags", (128, 4))

    out = nc.dram_tensor("out", [TPC, D], F32, kind="ExternalOutput").ap()

    dram = {}

    def idram(name, shape, dt=F32):
        dram[name] = nc.dram_tensor(name, list(shape), dt).ap()

    idram("xcur", (TPC, D))
    idram("zp", (NSEG * 128, D), BF16)
    idram("tern1", (128, 32768), BF16)
    idram("tern2", (128, 32768), BF16)
    idram("kT_own", (8, 128, TPC), BF16)
    idram("kT_all", (4, 8, 128, TPC), BF16)
    idram("v_own", (TPC, D), BF16)
    idram("v_all", (4, TPC, D), BF16)
    idram("h_own", (TPC, D), BF16)
    idram("h_lrev", (TPC, D), BF16)
    idram("h_all", (4, TPC, D), BF16)
    idram("h_rev", (4, TPC, D), BF16)
    idram("ln0_own", (TPC, D), BF16)
    idram("xT_pre", (1024, S), BF16)
    idram("xcT", (1024, S), BF16)
    idram("zT", (512, S), BF16)
    idram("dtT", (512, S), BF16)
    idram("bcT", (32, S), BF16)
    idram("ygT", (512, S), BF16)
    idram("mamb_in", (S, D))
    idram("mamb_out", (TPC, D))
    idram("accd", (TPC, D))
    idram("pooled_own", (16, D), BF16)
    idram("pooled_all", (64, D), BF16)
    idram("edges_own", (256, 2048), BF16)
    idram("edges_all", (4, 256, 2048), BF16)

    with ExitStack() as ctx:
        cm = Common()
        cm.PA = ctx.enter_context(nc.psum_tensor("PA", [128, 2048], F32))
        cm.PB = ctx.enter_context(nc.psum_tensor("PB", [128, 1024], F32))
        cm.PC = ctx.enter_context(nc.psum_tensor("PC", [128, 512], F32))
        cm.PD = ctx.enter_context(nc.psum_tensor("PD", [128, 512], F32))
        cm.IDT = ctx.enter_context(nc.sbuf_tensor("IDT", [128, 128], BF16))
        cm.COL = ctx.enter_context(nc.sbuf_tensor("COL", [128, 192], F32))
        cm.TMP = ctx.enter_context(nc.sbuf_tensor("TMP", [128, 1024], F32))
        cm.DMP = ctx.enter_context(nc.sbuf_tensor("DMP", [128, 4096], F32))
        cm.ONEB = ctx.enter_context(nc.sbuf_tensor("ONEB", [128, 2], BF16))
        cm.EPSC = ctx.enter_context(nc.sbuf_tensor("EPSC", [128, 1], F32))
        block = ctx.enter_context(nc.Block())
        p = P(nc, block, ctx)

        p.dma("sync", cm.IDT[:], T["ident"])
        p.run("vector", lambda e: e.memset(cm.ONEB[:], 1.0))
        p.run("vector", lambda e: e.memset(cm.EPSC[:], EPS))

        if nsub < 1:
            pass
        if nsub >= 1:
            emit_sub1(p, cm, T, dram)
        if nsub >= 2:
            emit_ffn(p, cm, T, dram, first=True)
        if nsub >= 3:
            emit_sub3(p, cm, T, dram)
        if nsub >= 4:
            emit_ffn(p, cm, T, dram, first=False)
        if nsub >= 5:
            emit_sub5(p, cm, T, dram)
        if nsub >= 6:
            emit_moe(p, cm, T, dram, first=True)
        if nsub >= 7:
            emit_sub7(p, cm, T, dram)
        if nsub >= 8:
            emit_moe(p, cm, T, dram, first=False, final_out=out)
        else:
            for i in range(NT):
                p.dma("sync", cm.TMP[:], dram["xcur"][128 * i:128 * (i + 1), :])
                p.dma("gpsimd", out[128 * i:128 * (i + 1), :], cm.TMP[:])
        p.finish()
    return nc


def _rep(v, rows=128):
    v = np.asarray(v, np.float32)
    return np.ascontiguousarray(np.broadcast_to(v.reshape(1, -1), (rows, v.size)))


def kernel(**inputs):
    return _kernel_impl(inputs, nsub=8)


def _kernel_impl(inputs, nsub=8):
    import ml_dtypes
    bf = lambda a: np.ascontiguousarray(a).astype(ml_dtypes.bfloat16)
    f32 = lambda a: np.ascontiguousarray(np.asarray(a), dtype=np.float32)

    xin = f32(inputs["L0_attn_a_x"])
    lp2tok = build_lp2tok()

    ident = bf(np.eye(128, dtype=np.float32))
    jex = bf(np.eye(128, dtype=np.float32)[::-1].copy())
    m_rep = np.zeros((16, 128, 128), np.float32)
    for ph in range(16):
        for r in range(128):
            m_rep[ph, 8 * ph + r // 16, r] = 1.0
    m_rb = np.zeros((16, 128), np.float32)
    for r in range(128):
        m_rb[r % 16, r] = 1.0
    m_ssum = np.zeros((128, 8), np.float32)
    for r in range(128):
        m_ssum[r, r // 16] = 1.0
    bm = np.full((128, 384), NEG, np.float32)
    for pp in range(128):
        bm[pp, pp + 1:pp + 257] = 0.0

    common = {
        "ident": ident, "jex": jex,
        "m_rep": bf(m_rep), "m_rb": bf(m_rb), "m_ssum": bf(m_ssum),
        "wqkv": f32(inputs["L0_attn_a_W_qkv"]),
        "wout1": f32(inputs["L0_attn_a_W_out"]),
        "w1t": f32(np.asarray(inputs["L0_ffn_W1"]).T),
        "b1r": _rep(inputs["L0_ffn_b1"]),
        "w2t": f32(np.asarray(inputs["L0_ffn_W2"]).T),
        "b2r": _rep(inputs["L0_ffn_b2"]),
        "wq3": f32(np.asarray(inputs["L0_attn_b_Wq"]).T),
        "wk3": f32(np.asarray(inputs["L0_attn_b_Wk"]).T),
        "wv3": f32(np.asarray(inputs["L0_attn_b_Wv"]).T),
        "bq3r": _rep(inputs["L0_attn_b_bq"]),
        "bk3r": _rep(inputs["L0_attn_b_bk"]),
        "bv3r": _rep(inputs["L0_attn_b_bv"]),
        "lngr": _rep(inputs["L1_attn_a_ln_g"]),
        "lnbr": _rep(inputs["L1_attn_a_ln_b"]),
        "m_wx_f": None,
        "w_expT": f32(np.asarray(inputs["L1_ffn_W_experts"]).transpose(0, 2, 1)),
        "b_exp": f32(inputs["L1_ffn_b_experts"]),
        "w_rT": f32(np.asarray(inputs["L1_ffn_W_router"]).T),
        "b_rr": _rep(inputs["L1_ffn_b_router"]),
        "wq7": f32(inputs["L1_attn_b_wq"]),
        "wk7": f32(inputs["L1_attn_b_wk"]),
        "wv7": f32(inputs["L1_attn_b_wv"]),
        "wgq7": f32(inputs["L1_attn_b_wgq"]),
        "wgk7": f32(inputs["L1_attn_b_wgk"]),
        "wgv7": f32(inputs["L1_attn_b_wgv"]),
        "row7": f32(inputs["L1_attn_b_r_out_w"]),
        "wout7": f32(inputs["L1_attn_b_w_out"]),
        "glwr": _rep(np.asarray(inputs["L1_attn_b_gate_local_w"]).reshape(-1)),
        "grwr": _rep(np.asarray(inputs["L1_attn_b_gate_remote_w"]).reshape(-1)),
        "bm": bm,
    }
    del common["m_wx_f"]

    in_maps = []
    for c in range(8):
        b, q = c // 4, c % 4
        dirn, half = q // 2, q % 2
        sl0 = 512 * half
        pre = "m1" if dirn == 0 else "m2"
        Win = f32(inputs[f"L1_attn_a_{pre}_Win"])
        Wconv = f32(inputs[f"L1_attn_a_{pre}_Wconv"])
        bconv = f32(inputs[f"L1_attn_a_{pre}_bconv"])
        Wx = f32(inputs[f"L1_attn_a_{pre}_Wx"])
        Wdt = f32(inputs[f"L1_attn_a_{pre}_Wdt"])
        bdt = f32(inputs[f"L1_attn_a_{pre}_bdt"])
        Alog = f32(inputs[f"L1_attn_a_{pre}_Alog"])
        Dp = f32(inputs[f"L1_attn_a_{pre}_Dp"])
        Wout = f32(inputs[f"L1_attn_a_{pre}_Wout"])

        xap, xkv = make_xap_xkv(q, b, xin, lp2tok)
        selo = make_selo(q, lp2tok)

        m_wcvr = np.zeros((128, 32), np.float32)
        m_bcvr = np.zeros((128, 8), np.float32)
        for dc in range(8):
            m_wcvr[:, 4 * dc:4 * (dc + 1)] = Wconv[128 * dc:128 * (dc + 1), :]
            m_bcvr[:, dc] = bconv[128 * dc:128 * (dc + 1)]
        m_bdtr = np.zeros((128, 4), np.float32)
        m_dpr = np.zeros((128, 4), np.float32)
        for mcl in range(4):
            m_bdtr[:, mcl] = bdt[sl0 + 128 * mcl: sl0 + 128 * (mcl + 1)]
            m_dpr[:, mcl] = Dp[sl0 + 128 * mcl: sl0 + 128 * (mcl + 1)]
        m_alogr = np.zeros((128, 64), np.float32)
        for g in range(64):
            for r in range(128):
                m_alogr[r, g] = Alog[sl0 + 8 * g + r // 16, r % 16]
        m_woutp = np.zeros((512, 1024), np.float32)
        m_woutp[:, 512 * dirn:512 * (dirn + 1)] = Wout[sl0:sl0 + 512, :]
        mflags = np.zeros((128, 4), np.float32)
        mflags[:, 0] = 1.0 if dirn == 0 else 0.0
        mflags[:, 1] = 1.0 - mflags[:, 0]
        mflags[:, 2] = 1.0 if half == 0 else 0.0
        mflags[:, 3] = 1.0 - mflags[:, 2]

        bm0 = common["bm"].copy()
        bm1 = common["bm"].copy()
        if q == 0:
            bm0[:, 0:256] = NEG
            bm1[:, 0:128] = NEG
        lmask = np.zeros((NT, 128, 64), np.float32)
        for i in range(NT):
            lim = 16 * q + i
            lmask[i, :, lim + 1:] = NEG
        eflags = np.zeros((128, 4), np.float32)
        if q > 0:
            eflags[:, q - 1] = 1.0

        m = dict(common)
        m.update({
            "x0": np.ascontiguousarray(xin[b, 2048 * q:2048 * (q + 1)]),
            "xap": xap, "xkv": xkv, "selo": bf(selo),
            "m_winx": np.ascontiguousarray(Win[:, 0:1024]),
            "m_winz": np.ascontiguousarray(Win[:, 1024 + sl0:1024 + sl0 + 512]),
            "m_wcvr": m_wcvr, "m_bcvr": m_bcvr,
            "m_wx": Wx,
            "m_wdt": np.ascontiguousarray(Wdt[:, sl0:sl0 + 512]),
            "m_bdtr": m_bdtr, "m_alogr": m_alogr, "m_dpr": m_dpr,
            "m_woutp": m_woutp, "mflags": mflags,
            "bm0": bm0, "bm1": bm1, "lmask": lmask, "eflags": eflags,
        })
        in_maps.append(m)

    nc = build_nc(nsub)
    res = run_bass_kernel_spmd(nc, in_maps, core_ids=list(range(8)))
    outa = np.empty((B, S, D), np.float32)
    for c in range(8):
        b, q = c // 4, c % 4
        outa[b, 2048 * q:2048 * (q + 1)] = res.results[c]["out"]
    return outa


# version 05 (final)
# speedup vs baseline: 1.0894x; 1.0894x over previous
import math
import numpy as np
from contextlib import ExitStack

import concourse.bass as bass
import concourse.mybir as mybir
from concourse.alu_op_type import AluOpType as OP
from concourse.bass_utils import run_bass_kernel_spmd

F32 = mybir.dt.float32
BF16 = mybir.dt.bfloat16
AF = mybir.ActivationFunctionType
AX = mybir.AxisListType

B, S, D = 2, 8192, 1024
TPC, NT = 2048, 16
EPS = 1e-5
G = 91
NSEG = 18
NEG = -1e9

GRP_BATCH = [[0, 1, 2, 3], [4, 5, 6, 7]]
GRP_REV = [[3, 2, 1, 0], [7, 6, 5, 4]]


def lp_of_token(t):
    r = t // G
    if r % 2 == 0:
        return t
    return 182 * r + 90 - t


def build_lp2tok():
    lp2tok = np.empty(S, dtype=np.int64)
    for t in range(S):
        lp2tok[lp_of_token(t)] = t
    return lp2tok


def make_selo(q, lp2tok):
    selo = np.zeros((NT, 384, 128), np.float32)
    for i in range(NT):
        for p in range(128):
            t = 2048 * q + 128 * i + p
            lp = lp_of_token(t)
            j = lp // 128 - (16 * q - 1)
            jj = j - i
            assert 0 <= jj < 3, (q, i, p, jj)
            selo[i, jj * 128 + (lp % 128), p] = 1.0
    return selo


def make_xap_xkv(q, b, xin, lp2tok):
    xap = np.zeros((NSEG * 128, D), np.float32)
    xkv = np.zeros((NSEG * 64, D), np.float32)
    for j in range(NSEG):
        sg = 16 * q - 1 + j
        if sg < 0 or sg >= 64:
            continue
        toks = lp2tok[128 * sg:128 * sg + 128]
        xap[128 * j:128 * (j + 1)] = xin[b, toks]
        kv = toks[0::2]
        xkv[64 * j:64 * (j + 1)] = xin[b, kv]
    return xap, xkv


ENGS = ("tensor", "vector", "scalar", "gpsimd", "sync")


class P:
    def __init__(self, nc, block, ctx):
        import os
        self.limit = int(os.environ.get("PLIM", "1000000000"))
        self.ncalls = 0
        self.nc = nc
        self.block = block
        self.sem = {e: ctx.enter_context(nc.semaphore("S" + e)) for e in ENGS}
        self.dsem = {"sync": ctx.enter_context(nc.semaphore("Dsync")),
                     "gpsimd": ctx.enter_context(nc.semaphore("Dgp"))}
        self.csem = ctx.enter_context(nc.semaphore("Scc"))
        self.cnt = {e: 0 for e in ENGS}
        self.dcnt = {"sync": 0, "gpsimd": 0}
        self.ccnt = 0
        self.waited = {}

    def _waits(self, eng):
        out = []
        for e in ENGS:
            if e == eng:
                continue
            v = self.cnt[e]
            if v > self.waited.get((eng, "e" + e), 0):
                out.append((self.sem[e], v))
                self.waited[(eng, "e" + e)] = v
        for qn in ("sync", "gpsimd"):
            v = self.dcnt[qn]
            if v > self.waited.get((eng, "d" + qn), 0):
                out.append((self.dsem[qn], v))
                self.waited[(eng, "d" + qn)] = v
        v = self.ccnt
        if v > self.waited.get((eng, "cc"), 0):
            out.append((self.csem, v))
            self.waited[(eng, "cc")] = v
        return out

    def run(self, eng, fn):
        self.ncalls += 1
        if self.ncalls > self.limit:
            return
        waits = self._waits(eng)
        prev = self.cnt[eng]
        self.cnt[eng] += 1
        sem, val = self.sem[eng], self.cnt[eng]

        def body(e, waits=waits, fn=fn, sem=sem, prev=prev):
            if prev > 0:
                e.wait_ge(sem, prev)
            for s, v in waits:
                e.wait_ge(s, v)
            last = fn(e)
            last.then_inc(sem, 1)

        getattr(self.block, eng)(body)

    def dma(self, qn, out, in_):
        try:
            if out.tensor.dtype != in_.tensor.dtype:
                qn = "gpsimd"
        except AttributeError:
            pass
        self.ncalls += 1
        if self.ncalls > self.limit:
            return
        waits = self._waits(qn)
        self.dcnt[qn] += 16
        dsem = self.dsem[qn]

        def body(e, waits=waits, out=out, in_=in_, dsem=dsem):
            for s, v in waits:
                e.wait_ge(s, v)
            e.dma_start(out=out, in_=in_).then_inc(dsem, 16)

        getattr(self.block, qn)(body)

    def coll(self, kind, op, groups, ins, outs):
        self.ncalls += 1
        if self.ncalls > self.limit:
            return
        waits = self._waits("gpsimd")
        self.ccnt += 1
        csem = self.csem

        def body(e, waits=waits):
            for s, v in waits:
                e.wait_ge(s, v)
            e.collective_compute(kind, op, replica_groups=groups,
                                 ins=ins, outs=outs).then_inc(csem, 1)

        self.block.gpsimd(body)

    def finish(self):
        vs, vg, vc = self.dcnt["sync"], self.dcnt["gpsimd"], self.ccnt
        ss, sg, sc = self.dsem["sync"], self.dsem["gpsimd"], self.csem

        def body(e):
            e.wait_ge(ss, vs)
            e.wait_ge(sg, vg)
            if vc:
                e.wait_ge(sc, vc)

        self.block.gpsimd(body)


def MM(e, out, lhsT, rhs, start, stop):
    n = rhs.shape[-1]
    last = None
    for j in range(0, n, 512):
        w = min(512, n - j)
        last = e.matmul(out[:, j:j + w], lhsT, rhs[:, j:j + w],
                        start=start, stop=stop)
    return last


class Common:
    pass


def ln_tile(p, cm, src, dst, rows=128, affine=None):
    COL, DMP = cm.COL, cm.DMP
    r = rows

    p.run("vector", lambda e: e.reduce_sum(COL[0:r, 0:1], src, AX.X))
    p.run("scalar", lambda e: e.activation(COL[0:r, 1:2], COL[0:r, 0:1],
                                           AF.Identity, scale=1.0 / D))
    p.run("vector", lambda e: e.tensor_scalar(DMP[0:r, 0:1024], src,
                                              COL[0:r, 1:2], None, OP.subtract))
    p.run("scalar", lambda e: e.activation(DMP[0:r, 1024:2048], DMP[0:r, 0:1024],
                                           AF.Square, accum_out=COL[0:r, 2:3]))
    p.run("vector", lambda e: e.tensor_copy(COL[0:r, 3:4], COL[0:r, 2:3]))
    p.run("scalar", lambda e: e.activation(COL[0:r, 4:5], COL[0:r, 3:4],
                                           AF.Sqrt, bias=cm.EPSC[0:r, 0:1],
                                           scale=1.0 / D))
    p.run("vector", lambda e: e.reciprocal(COL[0:r, 5:6], COL[0:r, 4:5]))
    if affine is None:
        p.run("vector", lambda e: e.tensor_scalar(dst, DMP[0:r, 0:1024],
                                                  COL[0:r, 5:6], None, OP.mult))
    else:
        g_ap, b_ap = affine
        p.run("vector", lambda e: e.tensor_scalar(DMP[0:r, 1024:2048],
                                                  DMP[0:r, 0:1024],
                                                  COL[0:r, 5:6], None, OP.mult))
        p.run("vector", lambda e: e.tensor_tensor(DMP[0:r, 0:1024],
                                                  DMP[0:r, 1024:2048], g_ap,
                                                  OP.mult))
        p.run("vector", lambda e: e.tensor_tensor(dst, DMP[0:r, 0:1024], b_ap,
                                                  OP.add))


def tr8(p, cm, src_tm, dst_fm, PA):
    IDT = cm.IDT

    def t(e):
        last = None
        for c in range(8):
            last = e.matmul(PA[:, 128 * c:128 * (c + 1)],
                               src_tm[:, 128 * c:128 * (c + 1)], IDT[:], start=True, stop=True)
        return last
    p.run("tensor", t)
    p.run("vector", lambda e: e.tensor_copy(dst_fm, PA[:, 0:1024]))


def emit_sub1(p, cm, T, dram):
    nc = p.nc
    with ExitStack() as cx:
        sb = lambda n, sh, dt: cx.enter_context(nc.sbuf_tensor(n, sh, dt))
        WQKV = sb("s1_wqkv", [128, 24576], BF16)
        WOUT = sb("s1_wout", [128, 8192], BF16)
        SG = sb("s1_sg", [128, 8192], BF16)
        PA, PB = cm.PA, cm.PB
        TMP, COL = cm.TMP, cm.COL

        for pr in range(3):
            for dc in range(8):
                p.dma("sync", WQKV[:, (pr * 8 + dc) * 1024:(pr * 8 + dc + 1) * 1024],
                      T["wqkv"][128 * dc:128 * (dc + 1), 1024 * pr:1024 * (pr + 1)])
        for dc in range(8):
            p.dma("sync", WOUT[:, 1024 * dc:1024 * (dc + 1)],
                  T["wout1"][128 * dc:128 * (dc + 1), :])

        import os
        nseg_emit = int(os.environ.get("S1SEG", str(NSEG)))
        for j in range(nseg_emit):
            p.dma("sync", TMP[:], T["xap"][128 * j:128 * (j + 1), :])
            ln_tile(p, cm, TMP[:], SG[:, 0:1024])
            tr8(p, cm, SG[:, 0:1024], SG[:, 1024:2048], PA)
            p.dma("sync", TMP[0:64, :], T["xkv"][64 * j:64 * (j + 1), :])
            ln_tile(p, cm, TMP[0:64, :], SG[0:64, 2048:3072], rows=64)

            def tkv(e):
                last = None
                for c in range(8):
                    last = e.matmul(PA[:, 64 * c:64 * (c + 1)],
                                       SG[0:64, 2048 + 128 * c:2048 + 128 * (c + 1)],
                                       cm.IDT[0:64, 0:64], start=True, stop=True)
                return last
            p.run("tensor", tkv)
            p.run("vector", lambda e: e.tensor_copy(SG[:, 3072:3584], PA[:, 0:512]))

            def mq(e):
                last = None
                for dc in range(8):
                    last = MM(e, PB, SG[:, 1024 + 128 * dc:1024 + 128 * (dc + 1)],
                              WQKV[:, dc * 1024:(dc + 1) * 1024],
                              dc == 0, dc == 7)
                return last
            p.run("tensor", mq)
            p.run("vector", lambda e: e.tensor_copy(SG[:, 0:1024], PB[:]))

            def tq(e):
                last = None
                for h in range(16):
                    last = e.matmul(PA[0:64, 128 * h:128 * (h + 1)],
                                    SG[:, 64 * h:64 * (h + 1)],
                                    cm.IDT[:], start=True, stop=True)
                return last
            p.run("tensor", tq)
            p.run("vector", lambda e: e.tensor_copy(SG[0:64, 4096:6144],
                                                    PA[0:64, 0:2048]))

            def mk(e):
                last = None
                for dc in range(8):
                    last = MM(e, PB[0:64, :], SG[:, 3072 + 64 * dc:3072 + 64 * (dc + 1)],
                              WQKV[:, (8 + dc) * 1024:(9 + dc) * 1024],
                              dc == 0, dc == 7)
                return last
            p.run("tensor", mk)
            p.run("vector", lambda e: e.tensor_copy(SG[0:64, 6144:7168], PB[0:64, :]))

            def tk(e):
                last = None
                for h in range(16):
                    last = e.matmul(PA[0:64, 64 * h:64 * (h + 1)],
                                    SG[0:64, 6144 + 64 * h:6144 + 64 * (h + 1)],
                                    cm.IDT[0:64, 0:64], start=True, stop=True)
                return last
            p.run("tensor", tk)
            p.run("vector", lambda e: e.tensor_copy(SG[0:64, 1024:2048],
                                                    PA[0:64, 0:1024]))

            def mv(e):
                last = None
                for dc in range(8):
                    last = MM(e, PB[0:64, :], SG[:, 3072 + 64 * dc:3072 + 64 * (dc + 1)],
                              WQKV[:, (16 + dc) * 1024:(17 + dc) * 1024],
                              dc == 0, dc == 7)
                return last
            p.run("tensor", mv)
            p.run("vector", lambda e: e.tensor_copy(SG[0:64, 7168:8192], PB[0:64, :]))

            def ms(e):
                last = None
                for h in range(16):
                    last = e.matmul(PA[:, 64 * h:64 * (h + 1)],
                                    SG[0:64, 4096 + 128 * h:4096 + 128 * (h + 1)],
                                    SG[0:64, 1024 + 64 * h:1024 + 64 * (h + 1)],
                                    start=True, stop=True)
                return last
            p.run("tensor", ms)

            def sexp1(e):
                last = None
                for h in range(16):
                    last = e.activation(SG[:, 6144 + 64 * h:6144 + 64 * (h + 1)],
                                        PA[:, 64 * h:64 * (h + 1)], AF.Exp,
                                        scale=0.125, accum_out=COL[:, 8 + h:9 + h])
                return last
            p.run("scalar", sexp1)

            def vrc1(e):
                last = None
                for h in range(16):
                    last = e.reciprocal(COL[:, 24 + h:25 + h], COL[:, 8 + h:9 + h])
                return last
            p.run("vector", vrc1)

            def tp(e):
                last = None
                for h in range(16):
                    last = e.matmul(PA[0:64, 128 * h:128 * (h + 1)],
                                       SG[:, 6144 + 64 * h:6144 + 64 * (h + 1)],
                                       cm.IDT[:], start=True, stop=True)
                return last
            p.run("tensor", tp)
            p.run("vector", lambda e: e.tensor_copy(SG[0:64, 2048:4096], PA[0:64, 0:2048]))

            def mpv(e):
                last = None
                for h in range(16):
                    last = e.matmul(PB[:, 64 * h:64 * (h + 1)],
                                    SG[0:64, 2048 + 128 * h:2048 + 128 * (h + 1)],
                                    SG[0:64, 7168 + 64 * h:7168 + 64 * (h + 1)],
                                    start=True, stop=True)
                return last
            p.run("tensor", mpv)

            def ysc(e):
                last = None
                for h in range(16):
                    last = e.tensor_scalar(SG[:, 64 * h:64 * (h + 1)],
                                           PB[:, 64 * h:64 * (h + 1)],
                                           COL[:, 24 + h:25 + h], None, OP.mult)
                return last
            p.run("vector", ysc)
            tr8(p, cm, SG[:, 0:1024], SG[:, 1024:2048], PA)

            def mz(e):
                last = None
                for dc in range(8):
                    last = MM(e, PB, SG[:, 1024 + 128 * dc:1024 + 128 * (dc + 1)],
                              WOUT[:, 1024 * dc:1024 * (dc + 1)], dc == 0, dc == 7)
                return last
            p.run("tensor", mz)
            p.run("vector", lambda e: e.tensor_copy(SG[:, 5120:6144], PB[:]))
            p.dma("gpsimd", dram["zp"][128 * j:128 * (j + 1), :], SG[:, 5120:6144])

        for i in range(NT if nseg_emit == NSEG else 0):
            p.dma("sync", SG[:, 3072:3456].rearrange("p (a q) -> p a q", a=3),
                  T["selo"][i].rearrange("(a p) q -> p a q", p=128))
            for a in range(3):
                p.dma("sync", SG[:, 1024 * a:1024 * (a + 1)],
                      dram["zp"][128 * (i + a):128 * (i + a + 1), :])

            def mo(e, i=i):
                last = None
                for a in range(3):
                    last = MM(e, PB, SG[:, 3072 + 128 * a:3072 + 128 * (a + 1)],
                              SG[:, 1024 * a:1024 * (a + 1)], a == 0, a == 2)
                return last
            p.run("tensor", mo)
            p.dma("sync", TMP[:], T["x0"][128 * i:128 * (i + 1), :])
            p.run("vector", lambda e: e.tensor_tensor(cm.DMP[:, 0:1024], TMP[:],
                                                      PB[:], OP.add))
            p.dma("gpsimd", dram["xcur"][128 * i:128 * (i + 1), :], cm.DMP[:, 0:1024])


def emit_ffn(p, cm, T, dram, first):
    nc = p.nc
    with ExitStack() as cx:
        sb = lambda n, sh, dt: cx.enter_context(nc.sbuf_tensor(n, sh, dt))
        tag = "a" if first else "b"
        WA = sb("f_wa" + tag, [128, 32768], BF16)
        WB = sb("f_wb" + tag, [128, 32768], BF16)
        D2 = sb("f_d2" + tag, [128, 3072], F32)
        BH = sb("f_bh" + tag, [128, 2048], F32)
        XB = sb("f_xb" + tag, [128, 1024], F32)
        XN = sb("f_xn" + tag, [128, 10240], BF16)
        PA, PB = cm.PA, cm.PB
        TMP, COL, DMP = cm.TMP, cm.COL, cm.DMP

        if first:
            for wi, wt in ((0, T["w1t"]), (1, T["w2t"])):
                nch = 8 if wi == 0 else 32
                wid = 4096 if wi == 0 else 1024
                nstrip = (nch * wid) // 2048
                for st in range(nstrip):
                    r0 = (st * 2048) // wid
                    c0 = (st * 2048) % wid
                    if wid >= 2048:
                        p.dma("sync", DMP[:, 0:2048],
                              wt[128 * r0:128 * (r0 + 1), c0:c0 + 2048])
                    else:
                        p.dma("sync",
                              DMP[:, 0:2048].rearrange("p (a n) -> p a n", a=2),
                              wt[128 * r0:128 * (r0 + 2), :].rearrange(
                                  "(a p) n -> p a n", p=128))
                    col = 8 + wi * 20 + (st % 20)
                    if st % 20 == st // 1 * 0 + (st % 20):
                        pass
                    p.run("scalar", lambda e, col=col, fresh=(st < 20): e.activation(
                        D2[:, 0:2048], DMP[:, 0:2048], AF.Abs,
                        accum_out=COL[:, 48 + (0 if wi == 0 else 1):49 + (0 if wi == 0 else 1)]))
                    a_col = 50 + wi * 2
                    b_col = 51 + wi * 2
                    if st == 0:
                        p.run("vector", lambda e, a_col=a_col, wi=wi: e.tensor_copy(
                            COL[:, a_col:a_col + 1], COL[:, 48 + wi:49 + wi]))
                    else:
                        src_c = a_col if st % 2 == 1 else b_col
                        dst_c = b_col if st % 2 == 1 else a_col
                        p.run("vector", lambda e, s=src_c, d=dst_c, wi=wi:
                              e.tensor_tensor(COL[:, d:d + 1], COL[:, s:s + 1],
                                              COL[:, 48 + wi:49 + wi], OP.add))
                fin = (50 + wi * 2) if (nstrip - 1) % 2 == 0 else (51 + wi * 2)
                p.run("vector", lambda e, fin=fin, wi=wi: e.tensor_copy(
                    COL[:, 54 + wi:55 + wi], COL[:, fin:fin + 1]))
            ONE = sb("f_one", [128, 129], F32)
            p.run("vector", lambda e: e.memset(ONE[:], 1.0))
            for wi in range(2):
                p.run("tensor", lambda e, wi=wi: e.matmul(
                    cm.PC[0:1, 0:1], ONE[:, 0:1], COL[:, 54 + wi:55 + wi],
                    start=True, stop=True))
                p.run("vector", lambda e: e.tensor_copy(ONE[0:1, 128:129], cm.PC[0:1, 0:1]))
                p.run("tensor", lambda e: e.matmul(
                    cm.PC[:, 0:1], ONE[0:1, 0:128], ONE[0:1, 128:129],
                    start=True, stop=True))
                p.run("vector", lambda e, wi=wi: e.tensor_scalar_mul(
                    COL[:, 56 + wi:57 + wi], cm.PC[:, 0:1], 1.0 / (4096.0 * 1024.0)))
                p.run("vector", lambda e, wi=wi: e.tensor_scalar_max(
                    COL[:, 58 + wi:59 + wi], COL[:, 56 + wi:57 + wi], 1e-8))
                p.run("vector", lambda e, wi=wi: e.tensor_scalar_mul(
                    COL[:, 60 + wi:61 + wi], COL[:, 58 + wi:59 + wi], 0.5))

            for wi, wt, dst in ((0, T["w1t"], WA), (1, T["w2t"], WB)):
                nch = 8 if wi == 0 else 32
                wid = 4096 if wi == 0 else 1024
                for c in range(nch):
                    for s0 in range(0, wid, 1024):
                        p.dma("sync", DMP[:, 0:1024],
                              wt[128 * c:128 * (c + 1), s0:s0 + 1024])
                        p.run("scalar", lambda e: e.activation(
                            D2[:, 0:1024], DMP[:, 0:1024], AF.Sign))
                        p.run("vector", lambda e: e.tensor_tensor(
                            D2[:, 1024:2048], DMP[:, 0:1024], D2[:, 0:1024], OP.mult))
                        p.run("vector", lambda e, wi=wi: e.tensor_scalar(
                            D2[:, 2048:3072], D2[:, 1024:2048],
                            COL[:, 60 + wi:61 + wi], None, OP.is_ge))
                        p.run("vector", lambda e, wi=wi, c=c, s0=s0, dst=dst, wid=wid:
                              e.tensor_tensor(dst[:, c * wid + s0:c * wid + s0 + 1024],
                                              D2[:, 2048:3072], D2[:, 0:1024], OP.mult))
            p.dma("gpsimd", dram["tern1"], WA[:])
            p.dma("gpsimd", dram["tern2"], WB[:])
        else:
            p.dma("sync", WA[:], dram["tern1"])
            p.dma("sync", WB[:], dram["tern2"])

        p.dma("sync", XB[:], T["b2r"])

        for i in range(NT):
            p.dma("sync", TMP[:], dram["xcur"][128 * i:128 * (i + 1), :])
            ln_tile(p, cm, TMP[:], XN[:, 0:1024])
            tr8(p, cm, XN[:, 0:1024], XN[:, 1024:2048], PA)
            for half in range(2):
                p.dma("sync", BH[:], T["b1r"][:, 2048 * half:2048 * (half + 1)])

                def mh(e, half=half):
                    last = None
                    for dc in range(8):
                        last = MM(e, PA, XN[:, 1024 + 128 * dc:1024 + 128 * (dc + 1)],
                                  WA[:, 4096 * dc + 2048 * half:4096 * dc + 2048 * (half + 1)],
                                  dc == 0, dc == 7)
                    return last
                p.run("tensor", mh)
                p.run("vector", lambda e: e.tensor_tensor(D2[:, 0:2048], PA[:, 0:2048],
                                                          BH[:], OP.add))
                p.run("scalar", lambda e, half=half: e.activation(
                    XN[:, 2048 + 2048 * half:2048 + 2048 * (half + 1)],
                    D2[:, 0:2048], AF.Gelu))
            for g in range(2):
                def th(e, g=g):
                    last = None
                    for c in range(16):
                        last = e.matmul(PA[:, 128 * c:128 * (c + 1)],
                                           XN[:, 2048 + 2048 * g + 128 * c:
                                              2048 + 2048 * g + 128 * (c + 1)], cm.IDT[:], start=True, stop=True)
                    return last
                p.run("tensor", th)
                p.run("vector", lambda e, g=g: e.tensor_copy(
                    XN[:, 6144 + 2048 * g:6144 + 2048 * (g + 1)], PA[:, 0:2048]))

            def mo(e):
                last = None
                for fc in range(32):
                    last = MM(e, PB, XN[:, 6144 + 128 * fc:6144 + 128 * (fc + 1)],
                              WB[:, 1024 * fc:1024 * (fc + 1)], fc == 0, fc == 31)
                return last
            p.run("tensor", mo)
            p.run("vector", lambda e: e.tensor_tensor(D2[:, 2048:3072], PB[:],
                                                      XB[:], OP.add))
            p.run("vector", lambda e: e.tensor_tensor(DMP[:, 0:1024], D2[:, 2048:3072],
                                                      TMP[:], OP.add))
            p.dma("gpsimd", dram["xcur"][128 * i:128 * (i + 1), :], DMP[:, 0:1024])


def emit_sub3(p, cm, T, dram):
    nc = p.nc
    TMP, COL, DMP = cm.TMP, cm.COL, cm.DMP
    PA, PB = cm.PA, cm.PB
    with ExitStack() as cx:
        sb = lambda n, sh, dt: cx.enter_context(nc.sbuf_tensor(n, sh, dt))
        QF = cx.enter_context(nc.sbuf_tensor("s3_qf", [128, 16384], BF16))
        with ExitStack() as cy:
            sb2 = lambda n, sh, dt: cy.enter_context(nc.sbuf_tensor(n, sh, dt))
            W3 = sb2("s3_w", [128, 24576], BF16)
            BR = sb2("s3_br", [128, 3072], F32)
            EX = sb2("s3_ex", [128, 4096], BF16)
            for pi, wn in enumerate(("wq3", "wk3", "wv3")):
                for dc in range(8):
                    p.dma("sync", W3[:, (pi * 8 + dc) * 1024:(pi * 8 + dc + 1) * 1024],
                          T[wn][128 * dc:128 * (dc + 1), :])
            p.dma("sync", BR[:, 0:1024], T["bq3r"])
            p.dma("sync", BR[:, 1024:2048], T["bk3r"])
            p.dma("sync", BR[:, 2048:3072], T["bv3r"])
            for i in range(NT):
                p.dma("sync", TMP[:], dram["xcur"][128 * i:128 * (i + 1), :])
                ln_tile(p, cm, TMP[:], EX[:, 0:1024])
                tr8(p, cm, EX[:, 0:1024], EX[:, 1024:2048], PA)
                for pi in range(3):
                    def mp(e, pi=pi):
                        last = None
                        for dc in range(8):
                            last = MM(e, PB, EX[:, 1024 + 128 * dc:1024 + 128 * (dc + 1)],
                                      W3[:, (pi * 8 + dc) * 1024:(pi * 8 + dc + 1) * 1024],
                                      dc == 0, dc == 7)
                        return last
                    p.run("tensor", mp)
                    p.run("vector", lambda e, pi=pi: e.tensor_tensor(
                        DMP[:, 0:1024], PB[:], BR[:, 1024 * pi:1024 * (pi + 1)], OP.add))
                    p.run("vector", lambda e: e.tensor_copy(EX[:, 2048:3072],
                                                            DMP[:, 0:1024]))
                    if pi == 0:
                        tr8(p, cm, EX[:, 2048:3072], QF[:, 1024 * i:1024 * (i + 1)], PA)
                    elif pi == 1:
                        tr8(p, cm, EX[:, 2048:3072], EX[:, 3072:4096], PA)
                        for dc in range(8):
                            p.dma("gpsimd", dram["kT_own"][dc, :, 128 * i:128 * (i + 1)],
                                  EX[:, 3072 + 128 * dc:3072 + 128 * (dc + 1)])
                    else:
                        p.dma("gpsimd", dram["v_own"][128 * i:128 * (i + 1), :],
                              EX[:, 2048:3072])
        p.coll("AllGather", OP.bypass, GRP_BATCH,
               [dram["kT_own"].opt()], [dram["kT_all"].opt()])
        p.coll("AllGather", OP.bypass, GRP_BATCH,
               [dram["v_own"].opt()], [dram["v_all"].opt()])

        with ExitStack() as cy:
            KQ = cy.enter_context(nc.sbuf_tensor("s3_kq", [128, 16384], BF16))
            VQ = cy.enter_context(nc.sbuf_tensor("s3_vq", [128, 16384], BF16))
            ACC = cy.enter_context(nc.sbuf_tensor("s3_acc", [128, 16384], F32))
            EX = cy.enter_context(nc.sbuf_tensor("s3_ex2", [128, 4096], BF16))
            for s in range(4):
                p.dma("sync", KQ[:].rearrange("p (a n) -> p a n", a=8),
                      dram["kT_all"][s].rearrange("a p n -> p a n"))
                p.dma("sync", VQ[:].rearrange("p (a n) -> p a n", a=16),
                      dram["v_all"][s].rearrange("(a p) n -> p a n", p=128))
                for i in range(NT):
                    def msc(e, i=i):
                        last = None
                        for dc in range(8):
                            last = MM(e, PA, QF[:, 1024 * i + 128 * dc:1024 * i + 128 * (dc + 1)],
                                      KQ[:, 2048 * dc:2048 * (dc + 1)], dc == 0, dc == 7)
                        return last
                    p.run("tensor", msc)
                    p.run("scalar", lambda e: e.activation(
                        EX[:, 0:2048], PA[:, 0:2048], AF.Exp, scale=1.0 / 32.0,
                        accum_out=COL[:, 32:33]))
                    if s == 0:
                        p.run("vector", lambda e, i=i: e.tensor_copy(
                            COL[:, 64 + i:65 + i], COL[:, 32:33]))
                    else:
                        sc = (64 + i) if s % 2 == 1 else (96 + i)
                        dc_ = (96 + i) if s % 2 == 1 else (64 + i)
                        p.run("vector", lambda e, sc=sc, dc_=dc_: e.tensor_tensor(
                            COL[:, dc_:dc_ + 1], COL[:, sc:sc + 1], COL[:, 32:33], OP.add))

                    def tpt(e):
                        last = None
                        for c in range(16):
                            last = e.matmul(PA[:, 128 * c:128 * (c + 1)],
                                               EX[:, 128 * c:128 * (c + 1)], cm.IDT[:], start=True, stop=True)
                        return last
                    p.run("tensor", tpt)
                    p.run("vector", lambda e: e.tensor_copy(EX[:, 2048:4096],
                                                            PA[:, 0:2048]))

                    def mpv(e):
                        last = None
                        for c in range(16):
                            last = MM(e, PB, EX[:, 2048 + 128 * c:2048 + 128 * (c + 1)],
                                      VQ[:, 1024 * c:1024 * (c + 1)], c == 0, c == 15)
                        return last
                    p.run("tensor", mpv)
                    if s == 0:
                        p.run("vector", lambda e, i=i: e.tensor_copy(
                            ACC[:, 1024 * i:1024 * (i + 1)], PB[:]))
                    else:
                        p.run("vector", lambda e, i=i: e.tensor_tensor(
                            DMP[:, 0:1024], ACC[:, 1024 * i:1024 * (i + 1)], PB[:], OP.add))
                        p.run("vector", lambda e, i=i: e.tensor_copy(
                            ACC[:, 1024 * i:1024 * (i + 1)], DMP[:, 0:1024]))
            for i in range(NT):
                fin = 96 + i
                p.run("vector", lambda e, fin=fin: e.reciprocal(
                    COL[:, 33:34], COL[:, fin:fin + 1]))
                p.run("vector", lambda e, i=i: e.tensor_scalar(
                    DMP[:, 0:1024], ACC[:, 1024 * i:1024 * (i + 1)],
                    COL[:, 33:34], None, OP.mult))
                p.dma("sync", TMP[:], dram["xcur"][128 * i:128 * (i + 1), :])
                p.run("vector", lambda e: e.tensor_tensor(DMP[:, 1024:2048],
                                                          DMP[:, 0:1024], TMP[:], OP.add))
                p.dma("gpsimd", dram["xcur"][128 * i:128 * (i + 1), :], DMP[:, 1024:2048])


def emit_sub5(p, cm, T, dram):
    nc = p.nc
    TMP, COL, DMP = cm.TMP, cm.COL, cm.DMP
    PA, PB = cm.PA, cm.PB

    with ExitStack() as cx:
        sb = lambda n, sh, dt: cx.enter_context(nc.sbuf_tensor(n, sh, dt))
        GR = sb("m_gr", [128, 2048], F32)
        XJ = sb("m_xj", [128, 3072], BF16)
        p.dma("sync", GR[:, 0:1024], T["lngr"])
        p.dma("sync", GR[:, 1024:2048], T["lnbr"])
        for i in range(NT):
            p.dma("sync", TMP[:], dram["xcur"][128 * i:128 * (i + 1), :])
            ln_tile(p, cm, TMP[:], XJ[:, 0:1024])
            p.run("vector", lambda e: e.tensor_copy(cm.DMP[:, 2048:3072], XJ[:, 0:1024]))
            ln_tile(p, cm, cm.DMP[:, 2048:3072], XJ[:, 1024:2048],
                    affine=(GR[:, 0:1024], GR[:, 1024:2048]))
            p.dma("gpsimd", dram["ln0_own"][128 * i:128 * (i + 1), :], XJ[:, 0:1024])
            p.dma("gpsimd", dram["h_own"][128 * i:128 * (i + 1), :], XJ[:, 1024:2048])
    p.coll("AllGather", OP.bypass, GRP_BATCH,
           [dram["h_own"].opt()], [dram["h_all"].opt()])

    with ExitStack() as cx:
        sb = lambda n, sh, dt: cx.enter_context(nc.sbuf_tensor(n, sh, dt))
        WIN = sb("m_win", [128, 12288], BF16)
        HB = sb("m_hb", [128, 5120], BF16)
        FL = sb("m_fl", [128, 4], F32)
        JEX = sb("m_jex", [128, 128], BF16)
        p.dma("sync", FL[:], T["mflags"])
        p.dma("sync", JEX[:], T["jex"])
        for dc in range(8):
            p.dma("sync", WIN[:, 1024 * dc:1024 * (dc + 1)],
                  T["m_winx"][128 * dc:128 * (dc + 1), :])
            p.dma("sync", WIN[:, 8192 + 512 * dc:8192 + 512 * (dc + 1)],
                  T["m_winz"][128 * dc:128 * (dc + 1), :])
        for j in range(64):
            s, ii = j // 16, j % 16
            jm = 63 - j
            sm, iim = jm // 16, jm % 16
            p.dma("sync", HB[:, 0:1024], dram["h_all"][s, 128 * ii:128 * (ii + 1), :])
            p.dma("sync", HB[:, 1024:2048],
                  dram["h_all"][sm, 128 * iim:128 * (iim + 1), :])
            p.run("tensor", lambda e: MM(e, PB, JEX[:], HB[:, 1024:2048], True, True))
            p.run("vector", lambda e: e.tensor_copy(HB[:, 1024:2048], PB[:]))
            p.run("vector", lambda e: e.tensor_scalar(DMP[:, 0:1024], HB[:, 0:1024],
                                                      FL[:, 0:1], None, OP.mult))
            p.run("vector", lambda e: e.scalar_tensor_tensor(
                HB[:, 2048:3072], HB[:, 1024:2048], FL[:, 1:2], DMP[:, 0:1024],
                OP.mult, OP.add))
            tr8(p, cm, HB[:, 2048:3072], HB[:, 3072:4096], PA)

            def mx(e):
                last = None
                for mc in range(8):
                    for dc in range(8):
                        last = e.matmul(PA[:, 128 * mc:128 * (mc + 1)],
                                        WIN[:, 1024 * dc + 128 * mc:1024 * dc + 128 * (mc + 1)],
                                        HB[:, 3072 + 128 * dc:3072 + 128 * (dc + 1)],
                                        start=(dc == 0), stop=(dc == 7))
                return last
            p.run("tensor", mx)
            p.run("vector", lambda e: e.tensor_copy(HB[:, 4096:5120], PA[:, 0:1024]))
            p.dma("gpsimd",
                  dram["xT_pre"][:, 128 * j:128 * (j + 1)]
                  .rearrange("(a p) n -> p a n", p=128),
                  HB[:, 4096:5120].rearrange("p (a n) -> p a n", a=8))

            def mzp(e):
                last = None
                for mc in range(4):
                    for dc in range(8):
                        last = e.matmul(PB[:, 128 * mc:128 * (mc + 1)],
                                        WIN[:, 8192 + 512 * dc + 128 * mc:
                                            8192 + 512 * dc + 128 * (mc + 1)],
                                        HB[:, 3072 + 128 * dc:3072 + 128 * (dc + 1)],
                                        start=(dc == 0), stop=(dc == 7))
                return last
            p.run("tensor", mzp)
            p.run("vector", lambda e: e.tensor_copy(HB[:, 4096:4608], PB[:, 0:512]))
            p.dma("gpsimd",
                  dram["zT"][:, 128 * j:128 * (j + 1)]
                  .rearrange("(a p) n -> p a n", p=128),
                  HB[:, 4096:4608].rearrange("p (a n) -> p a n", a=4))

    with ExitStack() as cx:
        sb = lambda n, sh, dt: cx.enter_context(nc.sbuf_tensor(n, sh, dt))
        CHB = sb("m_chb", [128, 8192], BF16)
        XP = sb("m_xp", [128, 8200], F32)
        CV1 = sb("m_cv1", [128, 8192], F32)
        CV2 = sb("m_cv2", [128, 8192], F32)
        WCV = sb("m_wcv", [128, 40], F32)
        p.dma("sync", WCV[:, 0:32], T["m_wcvr"])
        p.dma("sync", WCV[:, 32:40], T["m_bcvr"])
        for mc in range(8):
            p.dma("sync", CHB[:], dram["xT_pre"][128 * mc:128 * (mc + 1), :])
            p.run("vector", lambda e: e.memset(XP[:, 0:3], 0.0))
            p.run("vector", lambda e: e.tensor_copy(XP[:, 3:8195], CHB[:]))
            p.run("vector", lambda e, mc=mc: e.tensor_scalar(
                CV1[:], XP[:, 0:8192], WCV[:, 4 * mc:4 * mc + 1], None, OP.mult))
            for k in range(1, 4):
                src, dst = (CV1, CV2) if k % 2 == 1 else (CV2, CV1)
                p.run("vector", lambda e, mc=mc, k=k, src=src, dst=dst:
                      e.scalar_tensor_tensor(dst[:], XP[:, k:k + 8192],
                                             WCV[:, 4 * mc + k:4 * mc + k + 1],
                                             src[:], OP.mult, OP.add))
            p.run("vector", lambda e, mc=mc: e.tensor_scalar(
                CV1[:], CV2[:], WCV[:, 32 + mc:33 + mc], None, OP.add))
            p.run("scalar", lambda e: e.activation(CHB[:], CV1[:], AF.Silu))
            p.dma("gpsimd", dram["xcT"][128 * mc:128 * (mc + 1), :], CHB[:])

    with ExitStack() as cx:
        sb = lambda n, sh, dt: cx.enter_context(nc.sbuf_tensor(n, sh, dt))
        CHB = sb("m_chb2", [128, 8192], BF16)
        DA = sb("m_da", [64, 8192], F32)
        DB = sb("m_db", [64, 8192], F32)
        WX = sb("m_wxs", [128, 512], BF16)
        WDT = sb("m_wdts", [32, 512], BF16)
        BDT = sb("m_bdts", [128, 4], F32)
        DTB = sb("m_dtb", [32, 8192], BF16)
        BMB = sb("m_bmb", [16, 8192], BF16)
        CMB = sb("m_cmb", [16, 8192], BF16)
        EXQ = sb("m_exq", [128, 8192], BF16)
        for dc in range(8):
            p.dma("sync", WX[:, 64 * dc:64 * (dc + 1)],
                  T["m_wx"][128 * dc:128 * (dc + 1), :])
        p.dma("sync", WDT[:], T["m_wdt"])
        p.dma("sync", BDT[:], T["m_bdtr"])
        for mc in range(8):
            p.dma("sync", CHB[:], dram["xcT"][128 * mc:128 * (mc + 1), :])
            for tb in range(16):
                p.run("tensor", lambda e, mc=mc, tb=tb: e.matmul(
                    cm.PC[0:64, :], WX[:, 64 * mc:64 * (mc + 1)],
                    CHB[:, 512 * tb:512 * (tb + 1)], start=True, stop=True))
                if mc == 0:
                    p.run("vector", lambda e, tb=tb: e.tensor_copy(
                        DA[:, 512 * tb:512 * (tb + 1)], cm.PC[0:64, :]))
                else:
                    src, dst = (DA, DB) if mc % 2 == 1 else (DB, DA)
                    p.run("vector", lambda e, tb=tb, src=src, dst=dst:
                          e.tensor_tensor(dst[:, 512 * tb:512 * (tb + 1)],
                                          src[:, 512 * tb:512 * (tb + 1)],
                                          cm.PC[0:64, :], OP.add))
        FIN = DB
        p.dma("gpsimd", DTB[:], FIN[0:32, :])
        p.dma("gpsimd", BMB[:], FIN[32:48, :])
        p.dma("gpsimd", CMB[:], FIN[48:64, :])
        for mcl in range(4):
            for tb in range(16):
                p.run("tensor", lambda e, mcl=mcl, tb=tb: e.matmul(
                    cm.PC[:], WDT[:, 128 * mcl:128 * (mcl + 1)],
                    DTB[:, 512 * tb:512 * (tb + 1)], start=True, stop=True))
                p.run("scalar", lambda e, mcl=mcl, tb=tb: e.activation(
                    DMP[:, 0:512], cm.PC[:], AF.Exp, bias=BDT[:, mcl:mcl + 1],
                    scale=1.0))
                p.run("scalar", lambda e, tb=tb: e.activation(
                    EXQ[:, 512 * tb:512 * (tb + 1)], DMP[:, 0:512], AF.Ln,
                    bias=1.0))
            p.dma("gpsimd", dram["dtT"][128 * mcl:128 * (mcl + 1), :], EXQ[:])

        p.dma("gpsimd", dram["bcT"][0:16, :], BMB[:])
        p.dma("gpsimd", dram["bcT"][16:32, :], CMB[:])

    with ExitStack() as cy:
        sb2 = lambda n, sh, dt: cy.enter_context(nc.sbuf_tensor(n, sh, dt))
        DTC = sb2("m_dtc", [128, 8192], BF16)
        XC1 = sb2("m_xc1", [128, 8192], BF16)
        ZTC = sb2("m_ztc", [128, 8192], BF16)
        YC = sb2("m_yc", [128, 8192], F32)
        SC = sb2("m_sc", [128, 4096], F32)
        SCB = sb2("m_scb", [128, 1024], BF16)
        YGB = sb2("m_ygb", [128, 8192], BF16)
        REPS = sb2("m_repss", [128, 2048], BF16)
        RBS = sb2("m_rbss", [16, 128], BF16)
        SSUM = sb2("m_ssums", [128, 8], BF16)
        ALG = sb2("m_alg", [128, 192], F32)
        HST = sb2("m_hst", [128, 16], F32)
        FL = sb2("m_fl2", [128, 4], F32)
        DPC = sb2("m_dpc", [128, 4], F32)
        BMB = sb2("m_bmb2", [16, 8192], BF16)
        CMB = sb2("m_cmb2", [16, 8192], BF16)
        p.dma("sync", BMB[:], dram["bcT"][0:16, :])
        p.dma("sync", CMB[:], dram["bcT"][16:32, :])
        p.dma("sync", FL[:], T["mflags"])
        p.dma("sync", REPS[:].rearrange("p (a n) -> p a n", a=16),
              T["m_rep"].rearrange("a p n -> p a n"))
        p.dma("sync", RBS[:], T["m_rb"])
        p.dma("sync", SSUM[:], T["m_ssum"])
        p.dma("sync", ALG[:, 0:64], T["m_alogr"])
        p.dma("sync", DPC[:], T["m_dpr"])
        p.run("scalar", lambda e: e.activation(ALG[:, 64:128], ALG[:, 0:64], AF.Exp))
        p.run("vector", lambda e: e.tensor_scalar_mul(ALG[:, 128:192],
                                                      ALG[:, 64:128], -1.0))
        for mcl in range(4):
            p.dma("sync", DTC[:], dram["dtT"][128 * mcl:128 * (mcl + 1), :])
            p.dma("sync", XC1[:], dram["xcT"][128 * mcl:128 * (mcl + 1), :])
            p.dma("sync", YGB[:], dram["xcT"][512 + 128 * mcl:512 + 128 * (mcl + 1), :])
            p.run("vector", lambda e: e.tensor_scalar(SC[:, 0:4096], XC1[:, 0:4096],
                                                      FL[:, 2:3], None, OP.mult))
            p.run("vector", lambda e: e.scalar_tensor_tensor(
                XC1[:, 0:4096], YGB[:, 0:4096], FL[:, 3:4], SC[:, 0:4096],
                OP.mult, OP.add))
            p.run("vector", lambda e: e.tensor_scalar(SC[:, 0:4096], XC1[:, 4096:8192],
                                                      FL[:, 2:3], None, OP.mult))
            p.run("vector", lambda e: e.scalar_tensor_tensor(
                XC1[:, 4096:8192], YGB[:, 4096:8192], FL[:, 3:4], SC[:, 0:4096],
                OP.mult, OP.add))
            for gl in range(16):
                g = 16 * mcl + gl
                for tb in range(8):
                    t0, t1 = 1024 * tb, 1024 * (tb + 1)

                    def m1(e, gl=gl, t0=t0, t1=t1):
                        e.matmul(PA[:, 0:512], REPS[:, 128 * gl:128 * (gl + 1)],
                                 DTC[:, t0:t0 + 512], start=True, stop=True)
                        e.matmul(PA[:, 512:1024], REPS[:, 128 * gl:128 * (gl + 1)],
                                 DTC[:, t0 + 512:t1], start=True, stop=True)
                        e.matmul(PA[:, 1024:1536], REPS[:, 128 * gl:128 * (gl + 1)],
                                 XC1[:, t0:t0 + 512], start=True, stop=True)
                        e.matmul(PA[:, 1536:2048], REPS[:, 128 * gl:128 * (gl + 1)],
                                 XC1[:, t0 + 512:t1], start=True, stop=True)
                        e.matmul(PB[:, 0:512], RBS[:], BMB[:, t0:t0 + 512],
                                 start=True, stop=True)
                        return e.matmul(PB[:, 512:1024], RBS[:], BMB[:, t0 + 512:t1],
                                        start=True, stop=True)
                    p.run("tensor", m1)
                    p.run("scalar", lambda e, g=g: e.activation(
                        SC[:, 0:1024], PA[:, 0:1024], AF.Exp,
                        scale=ALG[:, 128 + g:129 + g]))
                    p.run("vector", lambda e: e.tensor_copy(SC[:, 1024:2048],
                                                            PB[:, 0:1024]))
                    p.run("vector", lambda e: e.tensor_tensor(
                        SC[:, 2048:3072], SC[:, 1024:2048], PA[:, 1024:2048], OP.mult))
                    p.run("vector", lambda e: e.tensor_tensor(
                        SC[:, 1024:2048], SC[:, 2048:3072], PA[:, 0:1024], OP.mult))
                    init = 0.0 if tb == 0 else HST[:, gl:gl + 1]
                    p.run("vector", lambda e, init=init: e.tensor_tensor_scan(
                        SC[:, 3072:4096], SC[:, 0:1024], SC[:, 1024:2048],
                        init, OP.mult, OP.add))
                    p.run("tensor", lambda e, t0=t0, t1=t1: (
                        e.matmul(PB[:, 0:512], RBS[:], CMB[:, t0:t0 + 512],
                                 start=True, stop=True),
                        e.matmul(PB[:, 512:1024], RBS[:], CMB[:, t0 + 512:t1],
                                 start=True, stop=True))[-1])
                    p.run("vector", lambda e, gl=gl: (
                        e.tensor_copy(HST[:, gl:gl + 1], SC[:, 4095:4096]),
                        e.tensor_tensor(SC[:, 0:1024], SC[:, 3072:4096],
                                        PB[:, 0:1024], OP.mult))[-1])
                    p.run("vector", lambda e: e.tensor_copy(SCB[:], SC[:, 0:1024]))
                    p.run("tensor", lambda e: (
                        e.matmul(cm.PC[0:8, :], SSUM[:], SCB[:, 0:512],
                                 start=True, stop=True),
                        e.matmul(cm.PD[0:8, :], SSUM[:], SCB[:, 512:1024],
                                 start=True, stop=True))[-1])
                    p.run("vector", lambda e: (
                        e.tensor_copy(SC[0:8, 3072:3584], cm.PC[0:8, :]),
                        e.tensor_copy(SC[0:8, 3584:4096], cm.PD[0:8, :]))[-1])
                    p.dma("gpsimd", YC[8 * gl:8 * gl + 8, t0:t0 + 1024],
                          SC[0:8, 3072:4096])
            p.dma("sync", ZTC[:], dram["zT"][128 * mcl:128 * (mcl + 1), :])
            for tb in range(8):
                t0 = 1024 * tb
                p.run("scalar", lambda e, t0=t0: e.activation(
                    SC[:, 0:1024], ZTC[:, t0:t0 + 1024], AF.Silu))
                p.run("vector", lambda e, t0=t0, mcl=mcl: e.scalar_tensor_tensor(
                    SC[:, 1024:2048], XC1[:, t0:t0 + 1024], DPC[:, mcl:mcl + 1],
                    YC[:, t0:t0 + 1024], OP.mult, OP.add))
                p.run("vector", lambda e, t0=t0: e.tensor_tensor(
                    YGB[:, t0:t0 + 1024], SC[:, 1024:2048], SC[:, 0:1024], OP.mult))
            p.dma("gpsimd", dram["ygT"][128 * mcl:128 * (mcl + 1), :], YGB[:])

    with ExitStack() as cx:
        sb = lambda n, sh, dt: cx.enter_context(nc.sbuf_tensor(n, sh, dt))
        WOP = sb("m_wop", [128, 4096], BF16)
        YGS = sb("m_ygs", [128, 512], BF16)
        for mcl in range(4):
            p.dma("sync", WOP[:, 1024 * mcl:1024 * (mcl + 1)],
                  T["m_woutp"][128 * mcl:128 * (mcl + 1), :])
        for tb in range(64):
            p.dma("sync", YGS[:].rearrange("p (a n) -> p a n", a=4),
                  dram["ygT"][:, 128 * tb:128 * (tb + 1)]
                  .rearrange("(a p) n -> p a n", p=128))

            def mw(e):
                last = None
                for mcl in range(4):
                    last = MM(e, PB, YGS[:, 128 * mcl:128 * (mcl + 1)],
                              WOP[:, 1024 * mcl:1024 * (mcl + 1)], mcl == 0, mcl == 3)
                return last
            p.run("tensor", mw)
            p.run("vector", lambda e: e.tensor_copy(DMP[:, 0:1024], PB[:]))
            p.dma("gpsimd", dram["mamb_in"][128 * tb:128 * (tb + 1), :], DMP[:, 0:1024])
    p.coll("ReduceScatter", OP.add, GRP_BATCH,
           [dram["mamb_in"].opt()], [dram["mamb_out"].opt()])

    with ExitStack() as cx:
        XJ = cx.enter_context(nc.sbuf_tensor("m_xj2", [128, 1024], BF16))
        for i in range(NT):
            p.dma("sync", TMP[:], dram["xcur"][128 * i:128 * (i + 1), :])
            p.dma("sync", DMP[:, 0:1024], dram["mamb_out"][128 * i:128 * (i + 1), :])
            p.dma("sync", XJ[:], dram["ln0_own"][128 * i:128 * (i + 1), :])
            p.run("vector", lambda e: e.tensor_copy(DMP[:, 1024:2048], XJ[:]))
            p.run("vector", lambda e: e.tensor_tensor(DMP[:, 2048:3072], DMP[:, 0:1024],
                                                      DMP[:, 1024:2048], OP.add))
            p.run("scalar", lambda e: e.activation(DMP[:, 0:1024], DMP[:, 2048:3072],
                                                   AF.Gelu))
            p.run("vector", lambda e: e.tensor_tensor(DMP[:, 1024:2048], DMP[:, 0:1024],
                                                      TMP[:], OP.add))
            p.dma("gpsimd", dram["xcur"][128 * i:128 * (i + 1), :], DMP[:, 1024:2048])


def emit_moe(p, cm, T, dram, first, final_out=None):
    nc = p.nc
    TMP, COL, DMP = cm.TMP, cm.COL, cm.DMP
    PA, PB, PC = cm.PA, cm.PB, cm.PC
    tag = "a" if first else "b"
    with ExitStack() as cx:
        sb = lambda n, sh, dt: cx.enter_context(nc.sbuf_tensor(n, sh, dt))
        WA = sb("e_wa" + tag, [128, 32768], BF16)
        XN = sb("e_xn" + tag, [128, 2048], BF16)
        RWA = sb("e_rw" + tag, [128, 128], F32)
        WRT = sb("e_wrt" + tag, [128, 64], BF16)
        BRR = sb("e_brr" + tag, [128, 8], F32)
        BEX = sb("e_bex" + tag, [16, 1024], BF16)
        RWT = sb("e_rwt" + tag, [16, 128], BF16)
        RW8 = sb("e_rw8" + tag, [128, 64], F32)
        for dc in range(8):
            p.dma("sync", WRT[:, 8 * dc:8 * (dc + 1)],
                  T["w_rT"][128 * dc:128 * (dc + 1), :])
        p.dma("sync", BRR[:], T["b_rr"])
        p.dma("sync", BEX[0:8, :], T["b_exp"])
        for grp in range(2):
            for ei in range(4):
                e_ = grp * 4 + ei
                for dc in range(8):
                    p.dma("sync", WA[:, (ei * 8 + dc) * 1024:(ei * 8 + dc + 1) * 1024],
                          T["w_expT"][e_, 128 * dc:128 * (dc + 1), :])
            for i in range(NT):
                p.dma("sync", TMP[:], dram["xcur"][128 * i:128 * (i + 1), :])
                ln_tile(p, cm, TMP[:], XN[:, 0:1024])
                tr8(p, cm, XN[:, 0:1024], XN[:, 1024:2048], PA)
                if grp == 0:
                    def mr(e):
                        last = None
                        for dc in range(8):
                            last = e.matmul(PC[:, 0:8],
                                            XN[:, 1024 + 128 * dc:1024 + 128 * (dc + 1)],
                                            WRT[:, 8 * dc:8 * (dc + 1)],
                                            start=(dc == 0), stop=(dc == 7))
                        return last
                    p.run("tensor", mr)

                    p.run("vector", lambda e: e.tensor_tensor(RW8[:, 0:8], PC[:, 0:8],
                                                              BRR[:], OP.add))
                    p.run("scalar", lambda e: e.activation(RW8[:, 8:16], RW8[:, 0:8],
                                                           AF.Exp))
                    p.run("vector", lambda e: e.reduce_sum(COL[:, 0:1], RW8[:, 8:16],
                                                           AX.X))
                    p.run("vector", lambda e: e.reciprocal(COL[:, 1:2], COL[:, 0:1]))
                    p.run("vector", lambda e: e.tensor_scalar(
                        RW8[:, 16:24], RW8[:, 8:16], COL[:, 1:2], None, OP.mult))
                    p.run("vector", lambda e: e.reduce_max(COL[:, 2:3], RW8[:, 16:24],
                                                           AX.X))
                    p.run("vector", lambda e: e.tensor_scalar(
                        RW8[:, 24:32], RW8[:, 16:24], COL[:, 2:3], None, OP.is_ge))
                    p.run("vector", lambda e: e.tensor_scalar_mul(
                        RW8[:, 32:40], RW8[:, 24:32], -1e9))
                    p.run("vector", lambda e: e.tensor_tensor(
                        RW8[:, 40:48], RW8[:, 16:24], RW8[:, 32:40], OP.add))
                    p.run("vector", lambda e: e.reduce_max(COL[:, 3:4], RW8[:, 40:48],
                                                           AX.X))
                    p.run("vector", lambda e: e.tensor_scalar(
                        RW8[:, 48:56], RW8[:, 16:24], COL[:, 3:4], None, OP.is_ge))
                    p.run("vector", lambda e, i=i: e.tensor_tensor(
                        RWA[:, 8 * i:8 * (i + 1)], RW8[:, 16:24], RW8[:, 48:56],
                        OP.mult))
                    cur = None
                else:
                    p.dma("sync", DMP[:, 2048:3072], dram["accd"][128 * i:128 * (i + 1), :])
                    cur = 2048
                for ei in range(4):
                    e_ = grp * 4 + ei

                    def me(e, ei=ei):
                        last = None
                        for dc in range(8):
                            last = MM(e, PB, XN[:, 1024 + 128 * dc:1024 + 128 * (dc + 1)],
                                      WA[:, (ei * 8 + dc) * 1024:(ei * 8 + dc + 1) * 1024],
                                      dc == 0, dc == 7)
                        return last
                    p.run("tensor", me)
                    if cur is None:
                        p.run("vector", lambda e, i=i, e_=e_: e.tensor_scalar(
                            DMP[:, 0:1024], PB[:], RWA[:, 8 * i + e_:8 * i + e_ + 1],
                            None, OP.mult))
                        cur = 0
                    else:
                        nxt = 1024 if cur != 1024 else 0
                        if grp == 1 and ei == 0:
                            nxt = 0
                        p.run("vector", lambda e, i=i, e_=e_, cur=cur, nxt=nxt:
                              e.scalar_tensor_tensor(
                                  DMP[:, nxt:nxt + 1024], PB[:],
                                  RWA[:, 8 * i + e_:8 * i + e_ + 1],
                                  DMP[:, cur:cur + 1024], OP.mult, OP.add))
                        cur = nxt
                if grp == 0:
                    p.dma("gpsimd", dram["accd"][128 * i:128 * (i + 1), :],
                          DMP[:, cur:cur + 1024])
                else:
                    p.run("vector", lambda e, i=i: e.tensor_copy(
                        XN[:, 1024:1032], RWA[:, 8 * i:8 * (i + 1)]))
                    p.run("tensor", lambda e: e.matmul(PC[0:8, 0:128],
                                                          XN[:, 1024:1032], cm.IDT[:], start=True, stop=True))
                    p.run("vector", lambda e: e.tensor_copy(RWT[0:8, :], PC[0:8, 0:128]))
                    p.run("tensor", lambda e: MM(e, PB, RWT[0:8, :], BEX[0:8, :],
                                                 True, True))
                    p.run("vector", lambda e, cur=cur: e.tensor_tensor(
                        DMP[:, 2048:3072], DMP[:, cur:cur + 1024], PB[:], OP.add))
                    p.run("vector", lambda e: e.tensor_tensor(
                        DMP[:, 0:1024], DMP[:, 2048:3072], TMP[:], OP.add))
                    dst = final_out if final_out is not None else dram["xcur"]
                    p.dma("gpsimd", dst[128 * i:128 * (i + 1), :], DMP[:, 0:1024])


def emit_sub7(p, cm, T, dram):
    nc = p.nc
    TMP, COL, DMP = cm.TMP, cm.COL, cm.DMP
    PA, PB, PC = cm.PA, cm.PB, cm.PC
    with ExitStack() as cx:
        sb = lambda n, sh, dt: cx.enter_context(nc.sbuf_tensor(n, sh, dt))
        KR = sb("a7_kr", [64, 6144], BF16)
        VR = sb("a7_vr", [128, 3072], BF16)
        W7 = sb("a7_w", [128, 32768], BF16)
        EX = sb("a7_ex", [128, 8192], BF16)
        HKH = sb("a7_hkh", [64, 4096], BF16)
        HVT = sb("a7_hvt", [128, 2048], BF16)
        GKH = sb("a7_gkh", [64, 512], BF16)
        GVT = sb("a7_gvt", [64, 512], BF16)
        MSK = sb("a7_msk", [128, 1216], F32)
        GLW = sb("a7_glw", [128, 2048], F32)
        LOC = sb("a7_loc", [128, 1024], F32)
        EFL = sb("a7_efl", [128, 4], F32)

        p.dma("sync", MSK[:, 0:384], T["bm"])
        p.dma("sync", MSK[:, 384:768], T["bm0"])
        p.dma("sync", MSK[:, 768:1152], T["bm1"])
        p.dma("sync", GLW[:, 0:1024], T["glwr"])
        p.dma("sync", GLW[:, 1024:2048], T["grwr"])
        p.dma("sync", EFL[:], T["eflags"])

        for dc in range(8):
            p.dma("sync", W7[:, 1024 * dc:1024 * (dc + 1)],
                  T["wk7"][128 * dc:128 * (dc + 1), :])
            p.dma("sync", W7[:, 8192 + 1024 * dc:8192 + 1024 * (dc + 1)],
                  T["wv7"][128 * dc:128 * (dc + 1), :])
        for i in range(NT):
            p.dma("sync", TMP[:], dram["xcur"][128 * i:128 * (i + 1), :])
            ln_tile(p, cm, TMP[:], EX[:, 2048:3072])
            tr8(p, cm, EX[:, 2048:3072], EX[:, 0:1024], PA)
            p.run("tensor", lambda e: (
                e.matmul(cm.PC[0:1, 0:512], cm.ONEB[:, 0:1],
                         EX[:, 2048:2560], start=True, stop=True),
                e.matmul(cm.PD[0:1, 0:512], cm.ONEB[:, 0:1],
                         EX[:, 2560:3072], start=True, stop=True))[-1])
            p.run("vector", lambda e: e.tensor_scalar_mul(
                DMP[0:1, 0:512], cm.PC[0:1, 0:512], 1.0 / 128))
            p.run("vector", lambda e: e.tensor_scalar_mul(
                DMP[0:1, 512:1024], cm.PD[0:1, 0:512], 1.0 / 128))
            p.run("vector", lambda e: e.tensor_copy(EX[0:1, 3072:4096],
                                                    DMP[0:1, 0:1024]))
            p.dma("gpsimd", dram["pooled_own"][i:i + 1, :], EX[0:1, 3072:4096])
            if i >= 14:
                for pi in range(2):
                    def mp(e, pi=pi):
                        last = None
                        for dc in range(8):
                            last = MM(e, PB, EX[:, 128 * dc:128 * (dc + 1)],
                                      W7[:, 8192 * pi + 1024 * dc:8192 * pi + 1024 * (dc + 1)],
                                      dc == 0, dc == 7)
                        return last
                    p.run("tensor", mp)
                    p.run("vector", lambda e, i=i, pi=pi: e.tensor_copy(
                        EX[:, 4096 + 2048 * (i - 14) + 1024 * pi:
                           4096 + 2048 * (i - 14) + 1024 * (pi + 1)], PB[:]))
        p.dma("gpsimd", dram["edges_own"][0:128, 0:1024], EX[:, 4096:5120])
        p.dma("gpsimd", dram["edges_own"][0:128, 1024:2048], EX[:, 5120:6144])
        p.dma("gpsimd", dram["edges_own"][128:256, 0:1024], EX[:, 6144:7168])
        p.dma("gpsimd", dram["edges_own"][128:256, 1024:2048], EX[:, 7168:8192])
        p.coll("AllGather", OP.bypass, GRP_BATCH,
               [dram["pooled_own"].opt()], [dram["pooled_all"].opt()])
        p.coll("AllGather", OP.bypass, GRP_BATCH,
               [dram["edges_own"].opt()], [dram["edges_all"].opt()])

        for dc in range(8):
            p.dma("sync", W7[:, 512 * dc:512 * (dc + 1)],
                  T["wgk7"][128 * dc:128 * (dc + 1), :])
            p.dma("sync", W7[:, 4096 + 512 * dc:4096 + 512 * (dc + 1)],
                  T["wgv7"][128 * dc:128 * (dc + 1), :])
        p.dma("sync", EX[0:64, 0:1024], dram["pooled_all"])

        def tpool(e):
            last = None
            for c in range(8):
                last = e.matmul(PA[:, 64 * c:64 * (c + 1)],
                                EX[0:64, 128 * c:128 * (c + 1)],
                                cm.IDT[0:64, 0:64], start=True, stop=True)
            return last
        p.run("tensor", tpool)
        p.run("vector", lambda e: e.tensor_copy(EX[:, 1024:1536], PA[:, 0:512]))
        for pi in range(2):
            def mg(e, pi=pi):
                last = None
                for dc in range(8):
                    last = MM(e, PB[0:64, 0:512], EX[:, 1024 + 64 * dc:1024 + 64 * (dc + 1)],
                              W7[:, 4096 * pi + 512 * dc:4096 * pi + 512 * (dc + 1)],
                              dc == 0, dc == 7)
                return last
            p.run("tensor", mg)
            if pi == 0:
                p.run("vector", lambda e: e.tensor_copy(EX[0:64, 1536:2048], PB[0:64, 0:512]))

                def tgk(e):
                    last = None
                    for lh in range(8):
                        last = e.matmul(PA[0:64, 64 * lh:64 * (lh + 1)],
                                        EX[0:64, 1536 + 64 * lh:1536 + 64 * (lh + 1)],
                                        cm.IDT[0:64, 0:64], start=True, stop=True)
                    return last
                p.run("tensor", tgk)
                p.run("vector", lambda e: e.tensor_copy(GKH[:], PA[0:64, 0:512]))
            else:
                p.run("vector", lambda e: e.tensor_copy(GVT[:], PB[0:64, 0:512]))
        for t in range(2):
            for s in range(4):
                p.dma("sync", EX[:, 2048 * (s % 2):2048 * (s % 2) + 2048],
                      dram["edges_all"][s, 128 * t:128 * (t + 1), :])
                if s == 0:
                    p.run("vector", lambda e: e.tensor_scalar(
                        DMP[:, 0:2048], EX[:, 0:2048], EFL[:, 0:1], None, OP.mult))
                else:
                    p.run("vector", lambda e, s=s: e.scalar_tensor_tensor(
                        DMP[:, 2048:4096] if s % 2 == 1 else DMP[:, 0:2048],
                        EX[:, 2048 * (s % 2):2048 * (s % 2) + 2048], EFL[:, s:s + 1],
                        DMP[:, 0:2048] if s % 2 == 1 else DMP[:, 2048:4096],
                        OP.mult, OP.add))
            p.run("vector", lambda e, t=t: e.tensor_copy(EX[:, 0:1024],
                                                         DMP[:, 2048:3072]))
            p.run("vector", lambda e, t=t: e.tensor_copy(
                HVT[:, 1024 * t:1024 * (t + 1)], DMP[:, 3072:4096]))

            def thk(e, t=t):
                last = None
                for h in range(16):
                    last = e.matmul(PA[0:64, 128 * h:128 * (h + 1)],
                                    EX[:, 64 * h:64 * (h + 1)],
                                    cm.IDT[:], start=True, stop=True)
                return last
            p.run("tensor", thk)
            p.run("vector", lambda e, t=t: e.tensor_copy(
                HKH[0:64, 2048 * t:2048 * (t + 1)], PA[0:64, 0:2048]))

        for dc in range(8):
            p.dma("sync", W7[:, 1024 * dc:1024 * (dc + 1)],
                  T["wq7"][128 * dc:128 * (dc + 1), :])
            p.dma("sync", W7[:, 8192 + 1024 * dc:8192 + 1024 * (dc + 1)],
                  T["wk7"][128 * dc:128 * (dc + 1), :])
            p.dma("sync", W7[:, 16384 + 1024 * dc:16384 + 1024 * (dc + 1)],
                  T["wv7"][128 * dc:128 * (dc + 1), :])
            p.dma("sync", W7[:, 24576 + 512 * dc:24576 + 512 * (dc + 1)],
                  T["wgq7"][128 * dc:128 * (dc + 1), :])
        WO7 = sb("a7_wo7", [128, 8192], BF16)
        for c in range(4):
            p.dma("sync", W7[:, 28672 + 1024 * c:28672 + 1024 * (c + 1)],
                  T["row7"][128 * c:128 * (c + 1), :])
        for dc in range(8):
            p.dma("sync", WO7[:, 1024 * dc:1024 * (dc + 1)],
                  T["wout7"][128 * dc:128 * (dc + 1), :])

        for i in range(NT):
            p.dma("sync", MSK[:, 1152:1216], T["lmask"][i])
            p.dma("sync", TMP[:], dram["xcur"][128 * i:128 * (i + 1), :])
            ln_tile(p, cm, TMP[:], EX[:, 6144:7168])
            tr8(p, cm, EX[:, 6144:7168], EX[:, 0:1024], PA)

            def mkr(e):
                last = None
                for dc in range(8):
                    last = MM(e, PB, EX[:, 128 * dc:128 * (dc + 1)],
                              W7[:, 8192 + 1024 * dc:8192 + 1024 * (dc + 1)],
                              dc == 0, dc == 7)
                return last
            p.run("tensor", mkr)
            p.run("vector", lambda e: e.tensor_copy(EX[:, 1024:2048], PB[:]))

            def tkr(e):
                last = None
                for h in range(16):
                    last = e.matmul(PA[0:64, 128 * h:128 * (h + 1)],
                                    EX[:, 1024 + 64 * h:1024 + 64 * (h + 1)],
                                    cm.IDT[:], start=True, stop=True)
                return last
            p.run("tensor", tkr)
            p.run("vector", lambda e, i=i: e.tensor_copy(
                KR[0:64, 2048 * (i % 3):2048 * (i % 3 + 1)], PA[0:64, 0:2048]))

            def mvr(e):
                last = None
                for dc in range(8):
                    last = MM(e, PB, EX[:, 128 * dc:128 * (dc + 1)],
                              W7[:, 16384 + 1024 * dc:16384 + 1024 * (dc + 1)],
                              dc == 0, dc == 7)
                return last
            p.run("tensor", mvr)
            p.run("vector", lambda e, i=i: e.tensor_copy(
                VR[:, 1024 * (i % 3):1024 * (i % 3 + 1)], PB[:]))

            def mq(e):
                last = None
                for dc in range(8):
                    last = MM(e, PB, EX[:, 128 * dc:128 * (dc + 1)],
                              W7[:, 1024 * dc:1024 * (dc + 1)], dc == 0, dc == 7)
                return last
            p.run("tensor", mq)
            p.run("vector", lambda e: e.tensor_tensor(DMP[:, 0:1024], PB[:],
                                                      GLW[:, 0:1024], OP.mult))
            p.run("vector", lambda e: e.tensor_reduce(
                COL[:, 0:16].rearrange("p (h o) -> p h o", o=1),
                DMP[:, 0:1024].rearrange("p (h k) -> p h k", k=64), AX.X, OP.add))
            p.run("vector", lambda e: e.tensor_tensor(DMP[:, 0:1024], PB[:],
                                                      GLW[:, 1024:2048], OP.mult))
            p.run("vector", lambda e: e.tensor_reduce(
                COL[:, 16:32].rearrange("p (h o) -> p h o", o=1),
                DMP[:, 0:1024].rearrange("p (h k) -> p h k", k=64), AX.X, OP.add))
            p.run("scalar", lambda e: e.activation(COL[:, 32:48], COL[:, 0:16],
                                                   AF.Sigmoid))
            p.run("scalar", lambda e: e.activation(COL[:, 48:64], COL[:, 16:32],
                                                   AF.Sigmoid))
            p.run("vector", lambda e: e.tensor_copy(EX[:, 1024:2048], PB[:]))

            def tqh(e):
                last = None
                for h in range(16):
                    last = e.matmul(PA[0:64, 128 * h:128 * (h + 1)],
                                    EX[:, 1024 + 64 * h:1024 + 64 * (h + 1)],
                                    cm.IDT[:], start=True, stop=True)
                return last
            p.run("tensor", tqh)
            p.run("vector", lambda e: e.tensor_copy(EX[0:64, 2048:4096],
                                                    PA[0:64, 0:2048]))

            moff = 384 if i == 0 else (768 if i == 1 else 0)
            ksrc, vsrc = [], []
            for a in range(3):
                bt = i - 2 + a
                if bt < 0:
                    ksrc.append((HKH, 2048 * (bt + 2)))
                    vsrc.append((HVT, 1024 * (bt + 2)))
                else:
                    ksrc.append((KR, 2048 * (bt % 3)))
                    vsrc.append((VR, 1024 * (bt % 3)))
            for hb in range(4):
                def msc(e, hb=hb, ksrc=ksrc):
                    last = None
                    for hh in range(4):
                        h = 4 * hb + hh
                        for a in range(3):
                            buf, off = ksrc[a]
                            last = e.matmul(
                                PA[:, 384 * hh + 128 * a:384 * hh + 128 * (a + 1)],
                                EX[0:64, 2048 + 128 * h:2048 + 128 * (h + 1)],
                                buf[0:64, off + 128 * h:off + 128 * (h + 1)],
                                start=True, stop=True)
                    return last
                p.run("tensor", msc)

                def vmask(e, moff=moff):
                    last = None
                    for hh in range(4):
                        last = e.tensor_tensor(DMP[:, 384 * hh:384 * (hh + 1)],
                                               PA[:, 384 * hh:384 * (hh + 1)],
                                               MSK[:, moff:moff + 384], OP.add)
                    return last
                p.run("vector", vmask)

                def sexp(e, hb=hb):
                    last = None
                    for hh in range(4):
                        last = e.activation(EX[:, 4096 + 384 * hh:4096 + 384 * (hh + 1)],
                                            DMP[:, 384 * hh:384 * (hh + 1)], AF.Exp,
                                            scale=0.125,
                                            accum_out=COL[:, 112 + 4 * hb + hh:113 + 4 * hb + hh])
                    return last
                p.run("scalar", sexp)

                def tpt(e):
                    last = None
                    for hh in range(4):
                        for a in range(3):
                            last = e.matmul(
                                PA[0:128, 512 * hh + 128 * a:512 * hh + 128 * (a + 1)],
                                EX[:, 4096 + 384 * hh + 128 * a:4096 + 384 * hh + 128 * (a + 1)],
                                cm.IDT[:], start=True, stop=True)
                    return last
                p.run("tensor", tpt)
                p.run("vector", lambda e: e.tensor_copy(EX[:, 5632:7680], PA[:, 0:2048]))

                def mpv(e, hb=hb, vsrc=vsrc):
                    last = None
                    for hh in range(4):
                        h = 4 * hb + hh
                        for a in range(3):
                            buf, off = vsrc[a]
                            last = e.matmul(PB[:, 64 * hh:64 * (hh + 1)],
                                            EX[:, 5632 + 512 * hh + 128 * a:
                                               5632 + 512 * hh + 128 * (a + 1)],
                                            buf[:, off + 64 * h:off + 64 * (h + 1)],
                                            start=(a == 0), stop=(a == 2))
                    return last
                p.run("tensor", mpv)

                def vrc(e, hb=hb):
                    last = None
                    for hh in range(4):
                        h = 4 * hb + hh
                        last = e.reciprocal(COL[:, 128 + h:129 + h],
                                            COL[:, 112 + h:113 + h])
                    return last
                p.run("vector", vrc)

                def vloc(e, hb=hb):
                    last = None
                    for hh in range(4):
                        h = 4 * hb + hh
                        last = e.tensor_scalar(LOC[:, 64 * h:64 * (h + 1)],
                                               PB[:, 64 * hh:64 * (hh + 1)],
                                               COL[:, 128 + h:129 + h], None, OP.mult)
                    return last
                p.run("vector", vloc)

            def mgq(e):
                last = None
                for dc in range(8):
                    last = MM(e, PB[:, 0:512], EX[:, 128 * dc:128 * (dc + 1)],
                              W7[:, 24576 + 512 * dc:24576 + 512 * (dc + 1)],
                              dc == 0, dc == 7)
                return last
            p.run("tensor", mgq)
            p.run("vector", lambda e: e.tensor_copy(EX[:, 1024:1536], PB[:, 0:512]))

            def tgq(e):
                last = None
                for lh in range(8):
                    last = e.matmul(PA[0:64, 128 * lh:128 * (lh + 1)],
                                    EX[:, 1024 + 64 * lh:1024 + 64 * (lh + 1)],
                                    cm.IDT[:], start=True, stop=True)
                return last
            p.run("tensor", tgq)
            p.run("vector", lambda e: e.tensor_copy(EX[0:64, 2048:3072],
                                                    PA[0:64, 0:1024]))

            def mls(e):
                last = None
                for lh in range(8):
                    last = e.matmul(PA[:, 64 * lh:64 * (lh + 1)],
                                    EX[0:64, 2048 + 128 * lh:2048 + 128 * (lh + 1)],
                                    GKH[0:64, 64 * lh:64 * (lh + 1)],
                                    start=True, stop=True)
                return last
            p.run("tensor", mls)

            def vlm(e):
                last = None
                for lh in range(8):
                    last = e.tensor_tensor(DMP[:, 64 * lh:64 * (lh + 1)],
                                           PA[:, 64 * lh:64 * (lh + 1)],
                                           MSK[:, 1152:1216], OP.add)
                return last
            p.run("vector", vlm)

            def slexp(e):
                last = None
                for lh in range(8):
                    last = e.activation(EX[:, 3072 + 64 * lh:3072 + 64 * (lh + 1)],
                                        DMP[:, 64 * lh:64 * (lh + 1)], AF.Exp,
                                        scale=0.125, accum_out=COL[:, 96 + lh:97 + lh])
                return last
            p.run("scalar", slexp)

            def tlp(e):
                last = None
                for lh in range(8):
                    last = e.matmul(PA[0:64, 128 * lh:128 * (lh + 1)],
                                    EX[:, 3072 + 64 * lh:3072 + 64 * (lh + 1)],
                                    cm.IDT[:], start=True, stop=True)
                return last
            p.run("tensor", tlp)
            p.run("vector", lambda e: e.tensor_copy(EX[0:64, 4096:5120], PA[0:64, 0:1024]))

            def mlv(e):
                last = None
                for lh in range(8):
                    last = e.matmul(PB[:, 512 + 64 * lh:512 + 64 * (lh + 1)],
                                    EX[0:64, 4096 + 128 * lh:4096 + 128 * (lh + 1)],
                                    GVT[0:64, 64 * lh:64 * (lh + 1)],
                                    start=True, stop=True)
                return last
            p.run("tensor", mlv)

            def vrr(e):
                last = None
                for lh in range(8):
                    last = e.reciprocal(COL[:, 144 + lh:145 + lh],
                                        COL[:, 96 + lh:97 + lh])
                return last
            p.run("vector", vrr)

            def vrem(e):
                last = None
                for lh in range(8):
                    last = e.tensor_scalar(EX[:, 1024 + 64 * lh:1024 + 64 * (lh + 1)],
                                           PB[:, 512 + 64 * lh:512 + 64 * (lh + 1)],
                                           COL[:, 144 + lh:145 + lh], None, OP.mult)
                return last
            p.run("vector", vrem)

            def trm(e):
                last = None
                for c in range(4):
                    last = e.matmul(PA[:, 128 * c:128 * (c + 1)],
                                    EX[:, 1024 + 128 * c:1024 + 128 * (c + 1)],
                                    cm.IDT[:], start=True, stop=True)
                return last
            p.run("tensor", trm)
            p.run("vector", lambda e: e.tensor_copy(EX[:, 1536:2048], PA[:, 0:512]))

            def mro(e):
                last = None
                for c in range(4):
                    last = MM(e, PB, EX[:, 1536 + 128 * c:1536 + 128 * (c + 1)],
                              W7[:, 28672 + 1024 * c:28672 + 1024 * (c + 1)],
                              c == 0, c == 3)
                return last
            p.run("tensor", mro)

            def vcmb1(e):
                last = None
                for h in range(16):
                    last = e.tensor_scalar(DMP[:, 64 * h:64 * (h + 1)],
                                           LOC[:, 64 * h:64 * (h + 1)],
                                           COL[:, 32 + h:33 + h], None, OP.mult)
                return last
            p.run("vector", vcmb1)

            def vcmb2(e):
                last = None
                for h in range(16):
                    last = e.scalar_tensor_tensor(
                        DMP[:, 1024 + 64 * h:1024 + 64 * (h + 1)],
                        PB[:, 64 * h:64 * (h + 1)], COL[:, 48 + h:49 + h],
                        DMP[:, 64 * h:64 * (h + 1)], OP.mult, OP.add)
                return last
            p.run("vector", vcmb2)
            p.run("vector", lambda e: e.tensor_copy(EX[:, 1024:2048], DMP[:, 1024:2048]))
            tr8(p, cm, EX[:, 1024:2048], EX[:, 0:1024], PA)

            def mwo(e):
                last = None
                for dc in range(8):
                    last = MM(e, PB, EX[:, 128 * dc:128 * (dc + 1)],
                              WO7[:, 1024 * dc:1024 * (dc + 1)], dc == 0, dc == 7)
                return last
            p.run("tensor", mwo)
            p.run("vector", lambda e: e.tensor_tensor(DMP[:, 0:1024], PB[:], TMP[:],
                                                      OP.add))
            p.dma("gpsimd", dram["xcur"][128 * i:128 * (i + 1), :], DMP[:, 0:1024])


def build_nc(nsub=8):
    nc = bass.Bass("TRN2", target_bir_lowering=False)

    def inp(name, shape, dt=F32):
        return nc.dram_tensor(name, list(shape), dt, kind="ExternalInput").ap()

    T = {}
    T["x0"] = inp("x0", (TPC, D))
    T["xap"] = inp("xap", (NSEG * 128, D))
    T["xkv"] = inp("xkv", (NSEG * 64, D))
    T["selo"] = inp("selo", (NT, 384, 128), BF16)
    T["wqkv"] = inp("wqkv", (D, 3 * D))
    T["wout1"] = inp("wout1", (D, D))
    T["w1t"] = inp("w1t", (D, 4096))
    T["b1r"] = inp("b1r", (128, 4096))
    T["w2t"] = inp("w2t", (4096, D))
    T["b2r"] = inp("b2r", (128, D))
    T["wq3"] = inp("wq3", (D, D))
    T["wk3"] = inp("wk3", (D, D))
    T["wv3"] = inp("wv3", (D, D))
    T["bq3r"] = inp("bq3r", (128, D))
    T["bk3r"] = inp("bk3r", (128, D))
    T["bv3r"] = inp("bv3r", (128, D))
    T["ident"] = inp("ident", (128, 128), BF16)
    T["jex"] = inp("jex", (128, 128), BF16)
    T["lngr"] = inp("lngr", (128, D))
    T["lnbr"] = inp("lnbr", (128, D))
    T["m_winx"] = inp("m_winx", (D, 1024))
    T["m_winz"] = inp("m_winz", (D, 512))
    T["m_wcvr"] = inp("m_wcvr", (128, 32))
    T["m_bcvr"] = inp("m_bcvr", (128, 8))
    T["m_wx"] = inp("m_wx", (D, 64))
    T["m_wdt"] = inp("m_wdt", (32, 512))
    T["m_bdtr"] = inp("m_bdtr", (128, 4))
    T["m_alogr"] = inp("m_alogr", (128, 64))
    T["m_dpr"] = inp("m_dpr", (128, 4))
    T["m_woutp"] = inp("m_woutp", (512, 1024))
    T["m_rep"] = inp("m_rep", (16, 128, 128), BF16)
    T["m_rb"] = inp("m_rb", (16, 128), BF16)
    T["m_ssum"] = inp("m_ssum", (128, 8), BF16)
    T["mflags"] = inp("mflags", (128, 4))
    T["w_expT"] = inp("w_expT", (8, D, D))
    T["b_exp"] = inp("b_exp", (8, D))
    T["w_rT"] = inp("w_rT", (D, 8))
    T["b_rr"] = inp("b_rr", (128, 8))
    T["wq7"] = inp("wq7", (D, D))
    T["wk7"] = inp("wk7", (D, D))
    T["wv7"] = inp("wv7", (D, D))
    T["wgq7"] = inp("wgq7", (D, 512))
    T["wgk7"] = inp("wgk7", (D, 512))
    T["wgv7"] = inp("wgv7", (D, 512))
    T["row7"] = inp("row7", (512, D))
    T["wout7"] = inp("wout7", (D, D))
    T["glwr"] = inp("glwr", (128, D))
    T["grwr"] = inp("grwr", (128, D))
    T["bm"] = inp("bm", (128, 384))
    T["bm0"] = inp("bm0", (128, 384))
    T["bm1"] = inp("bm1", (128, 384))
    T["lmask"] = inp("lmask", (NT, 128, 64))
    T["eflags"] = inp("eflags", (128, 4))

    out = nc.dram_tensor("out", [TPC, D], F32, kind="ExternalOutput").ap()

    dram = {}

    def idram(name, shape, dt=F32):
        dram[name] = nc.dram_tensor(name, list(shape), dt).ap()

    idram("xcur", (TPC, D))
    idram("zp", (NSEG * 128, D), BF16)
    idram("tern1", (128, 32768), BF16)
    idram("tern2", (128, 32768), BF16)
    idram("kT_own", (8, 128, TPC), BF16)
    idram("kT_all", (4, 8, 128, TPC), BF16)
    idram("v_own", (TPC, D), BF16)
    idram("v_all", (4, TPC, D), BF16)
    idram("h_own", (TPC, D), BF16)
    idram("h_lrev", (TPC, D), BF16)
    idram("h_all", (4, TPC, D), BF16)
    idram("h_rev", (4, TPC, D), BF16)
    idram("ln0_own", (TPC, D), BF16)
    idram("xT_pre", (1024, S), BF16)
    idram("xcT", (1024, S), BF16)
    idram("zT", (512, S), BF16)
    idram("dtT", (512, S), BF16)
    idram("bcT", (32, S), BF16)
    idram("ygT", (512, S), BF16)
    idram("mamb_in", (S, D))
    idram("mamb_out", (TPC, D))
    idram("accd", (TPC, D))
    idram("pooled_own", (16, D), BF16)
    idram("pooled_all", (64, D), BF16)
    idram("edges_own", (256, 2048), BF16)
    idram("edges_all", (4, 256, 2048), BF16)

    with ExitStack() as ctx:
        cm = Common()
        cm.PA = ctx.enter_context(nc.psum_tensor("PA", [128, 2048], F32))
        cm.PB = ctx.enter_context(nc.psum_tensor("PB", [128, 1024], F32))
        cm.PC = ctx.enter_context(nc.psum_tensor("PC", [128, 512], F32))
        cm.PD = ctx.enter_context(nc.psum_tensor("PD", [128, 512], F32))
        cm.IDT = ctx.enter_context(nc.sbuf_tensor("IDT", [128, 128], BF16))
        cm.COL = ctx.enter_context(nc.sbuf_tensor("COL", [128, 192], F32))
        cm.TMP = ctx.enter_context(nc.sbuf_tensor("TMP", [128, 1024], F32))
        cm.DMP = ctx.enter_context(nc.sbuf_tensor("DMP", [128, 4096], F32))
        cm.ONEB = ctx.enter_context(nc.sbuf_tensor("ONEB", [128, 2], BF16))
        cm.EPSC = ctx.enter_context(nc.sbuf_tensor("EPSC", [128, 1], F32))
        block = ctx.enter_context(nc.Block())
        p = P(nc, block, ctx)

        p.dma("sync", cm.IDT[:], T["ident"])
        p.run("vector", lambda e: e.memset(cm.ONEB[:], 1.0))
        p.run("vector", lambda e: e.memset(cm.EPSC[:], EPS))

        if nsub < 1:
            pass
        if nsub >= 1:
            emit_sub1(p, cm, T, dram)
        if nsub >= 2:
            emit_ffn(p, cm, T, dram, first=True)
        if nsub >= 3:
            emit_sub3(p, cm, T, dram)
        if nsub >= 4:
            emit_ffn(p, cm, T, dram, first=False)
        if nsub >= 5:
            emit_sub5(p, cm, T, dram)
        if nsub >= 6:
            emit_moe(p, cm, T, dram, first=True)
        if nsub >= 7:
            emit_sub7(p, cm, T, dram)
        if nsub >= 8:
            emit_moe(p, cm, T, dram, first=False, final_out=out)
        else:
            for i in range(NT):
                p.dma("sync", cm.TMP[:], dram["xcur"][128 * i:128 * (i + 1), :])
                p.dma("gpsimd", out[128 * i:128 * (i + 1), :], cm.TMP[:])
        p.finish()
    return nc


def _rep(v, rows=128):
    v = np.asarray(v, np.float32)
    return np.ascontiguousarray(np.broadcast_to(v.reshape(1, -1), (rows, v.size)))


def kernel(**inputs):
    return _kernel_impl(inputs, nsub=8)


def _kernel_impl(inputs, nsub=8):
    import ml_dtypes
    bf = lambda a: np.ascontiguousarray(a).astype(ml_dtypes.bfloat16)
    f32 = lambda a: np.ascontiguousarray(np.asarray(a), dtype=np.float32)

    xin = f32(inputs["L0_attn_a_x"])
    lp2tok = build_lp2tok()

    ident = bf(np.eye(128, dtype=np.float32))
    jex = bf(np.eye(128, dtype=np.float32)[::-1].copy())
    m_rep = np.zeros((16, 128, 128), np.float32)
    for ph in range(16):
        for r in range(128):
            m_rep[ph, 8 * ph + r // 16, r] = 1.0
    m_rb = np.zeros((16, 128), np.float32)
    for r in range(128):
        m_rb[r % 16, r] = 1.0
    m_ssum = np.zeros((128, 8), np.float32)
    for r in range(128):
        m_ssum[r, r // 16] = 1.0
    bm = np.full((128, 384), NEG, np.float32)
    for pp in range(128):
        bm[pp, pp + 1:pp + 257] = 0.0

    common = {
        "ident": ident, "jex": jex,
        "m_rep": bf(m_rep), "m_rb": bf(m_rb), "m_ssum": bf(m_ssum),
        "wqkv": f32(inputs["L0_attn_a_W_qkv"]),
        "wout1": f32(inputs["L0_attn_a_W_out"]),
        "w1t": f32(np.asarray(inputs["L0_ffn_W1"]).T),
        "b1r": _rep(inputs["L0_ffn_b1"]),
        "w2t": f32(np.asarray(inputs["L0_ffn_W2"]).T),
        "b2r": _rep(inputs["L0_ffn_b2"]),
        "wq3": f32(np.asarray(inputs["L0_attn_b_Wq"]).T),
        "wk3": f32(np.asarray(inputs["L0_attn_b_Wk"]).T),
        "wv3": f32(np.asarray(inputs["L0_attn_b_Wv"]).T),
        "bq3r": _rep(inputs["L0_attn_b_bq"]),
        "bk3r": _rep(inputs["L0_attn_b_bk"]),
        "bv3r": _rep(inputs["L0_attn_b_bv"]),
        "lngr": _rep(inputs["L1_attn_a_ln_g"]),
        "lnbr": _rep(inputs["L1_attn_a_ln_b"]),
        "m_wx_f": None,
        "w_expT": f32(np.asarray(inputs["L1_ffn_W_experts"]).transpose(0, 2, 1)),
        "b_exp": f32(inputs["L1_ffn_b_experts"]),
        "w_rT": f32(np.asarray(inputs["L1_ffn_W_router"]).T),
        "b_rr": _rep(inputs["L1_ffn_b_router"]),
        "wq7": f32(inputs["L1_attn_b_wq"]),
        "wk7": f32(inputs["L1_attn_b_wk"]),
        "wv7": f32(inputs["L1_attn_b_wv"]),
        "wgq7": f32(inputs["L1_attn_b_wgq"]),
        "wgk7": f32(inputs["L1_attn_b_wgk"]),
        "wgv7": f32(inputs["L1_attn_b_wgv"]),
        "row7": f32(inputs["L1_attn_b_r_out_w"]),
        "wout7": f32(inputs["L1_attn_b_w_out"]),
        "glwr": _rep(np.asarray(inputs["L1_attn_b_gate_local_w"]).reshape(-1)),
        "grwr": _rep(np.asarray(inputs["L1_attn_b_gate_remote_w"]).reshape(-1)),
        "bm": bm,
    }
    del common["m_wx_f"]

    in_maps = []
    for c in range(8):
        b, q = c // 4, c % 4
        dirn, half = q // 2, q % 2
        sl0 = 512 * half
        pre = "m1" if dirn == 0 else "m2"
        Win = f32(inputs[f"L1_attn_a_{pre}_Win"])
        Wconv = f32(inputs[f"L1_attn_a_{pre}_Wconv"])
        bconv = f32(inputs[f"L1_attn_a_{pre}_bconv"])
        Wx = f32(inputs[f"L1_attn_a_{pre}_Wx"])
        Wdt = f32(inputs[f"L1_attn_a_{pre}_Wdt"])
        bdt = f32(inputs[f"L1_attn_a_{pre}_bdt"])
        Alog = f32(inputs[f"L1_attn_a_{pre}_Alog"])
        Dp = f32(inputs[f"L1_attn_a_{pre}_Dp"])
        Wout = f32(inputs[f"L1_attn_a_{pre}_Wout"])

        xap, xkv = make_xap_xkv(q, b, xin, lp2tok)
        selo = make_selo(q, lp2tok)

        m_wcvr = np.zeros((128, 32), np.float32)
        m_bcvr = np.zeros((128, 8), np.float32)
        for dc in range(8):
            m_wcvr[:, 4 * dc:4 * (dc + 1)] = Wconv[128 * dc:128 * (dc + 1), :]
            m_bcvr[:, dc] = bconv[128 * dc:128 * (dc + 1)]
        m_bdtr = np.zeros((128, 4), np.float32)
        m_dpr = np.zeros((128, 4), np.float32)
        for mcl in range(4):
            m_bdtr[:, mcl] = bdt[sl0 + 128 * mcl: sl0 + 128 * (mcl + 1)]
            m_dpr[:, mcl] = Dp[sl0 + 128 * mcl: sl0 + 128 * (mcl + 1)]
        m_alogr = np.zeros((128, 64), np.float32)
        for g in range(64):
            for r in range(128):
                m_alogr[r, g] = Alog[sl0 + 8 * g + r // 16, r % 16]
        m_woutp = np.zeros((512, 1024), np.float32)
        m_woutp[:, 512 * dirn:512 * (dirn + 1)] = Wout[sl0:sl0 + 512, :]
        mflags = np.zeros((128, 4), np.float32)
        mflags[:, 0] = 1.0 if dirn == 0 else 0.0
        mflags[:, 1] = 1.0 - mflags[:, 0]
        mflags[:, 2] = 1.0 if half == 0 else 0.0
        mflags[:, 3] = 1.0 - mflags[:, 2]

        bm0 = common["bm"].copy()
        bm1 = common["bm"].copy()
        if q == 0:
            bm0[:, 0:256] = NEG
            bm1[:, 0:128] = NEG
        lmask = np.zeros((NT, 128, 64), np.float32)
        for i in range(NT):
            lim = 16 * q + i
            lmask[i, :, lim + 1:] = NEG
        eflags = np.zeros((128, 4), np.float32)
        if q > 0:
            eflags[:, q - 1] = 1.0

        m = dict(common)
        m.update({
            "x0": np.ascontiguousarray(xin[b, 2048 * q:2048 * (q + 1)]),
            "xap": xap, "xkv": xkv, "selo": bf(selo),
            "m_winx": np.ascontiguousarray(Win[:, 0:1024]),
            "m_winz": np.ascontiguousarray(Win[:, 1024 + sl0:1024 + sl0 + 512]),
            "m_wcvr": m_wcvr, "m_bcvr": m_bcvr,
            "m_wx": Wx,
            "m_wdt": np.ascontiguousarray(Wdt[:, sl0:sl0 + 512]),
            "m_bdtr": m_bdtr, "m_alogr": m_alogr, "m_dpr": m_dpr,
            "m_woutp": m_woutp, "mflags": mflags,
            "bm0": bm0, "bm1": bm1, "lmask": lmask, "eflags": eflags,
        })
        in_maps.append(m)

    nc = build_nc(nsub)
    res = run_bass_kernel_spmd(nc, in_maps, core_ids=list(range(8)))
    outa = np.empty((B, S, D), np.float32)
    for c in range(8):
        b, q = c // 4, c % 4
        outa[b, 2048 * q:2048 * (q + 1)] = res.results[c]["out"]
    return outa
